# version 02 (final)
# baseline (speedup 1.0000x reference)
import numpy as np
from contextlib import ExitStack
import concourse.bass as bass
import concourse.mybir as mybir
from concourse.bass_utils import run_bass_kernel_spmd

F32 = mybir.dt.float32
BF16 = mybir.dt.bfloat16
I32 = mybir.dt.int32
AF = mybir.ActivationFunctionType
OP = mybir.AluOpType
AX = mybir.AxisListType

NC = 8
B = 2
SEQ = 8192
D = 1024
TPC = 2048
NTT = TPC // 128


class Chain:
    def __init__(self):
        self.entries = []

    def op(self, eng, fn):
        self.entries.append((eng, fn, 1))

    def dma(self, out, in_, slow=False):
        self.entries.append(('sync',
            (lambda e, o=out, i=in_, s=slow: e.dma_start(out=o, in_=i, allow_slow_non_contiguous=s)), 16))

    def build(self, blk, S):
        cum = []
        tot = 0
        for (_, _, inc) in self.entries:
            cum.append(tot)
            tot += inc
        entries = self.entries

        def make(engname):
            def body(e):
                for i, (eng, fn, inc) in enumerate(entries):
                    if eng != engname:
                        continue
                    if i > 0:
                        e.wait_ge(S, cum[i])
                    fn(e).then_inc(S, inc)
                if engname == 'sync':
                    e.wait_ge(S, tot)
            return body
        blk.sync(make('sync'))
        blk.tensor(make('tensor'))
        blk.vector(make('vector'))
        blk.scalar(make('scalar'))
        blk.gpsimd(make('gpsimd'))


class L:
    def __init__(self, name):
        self.nc = bass.Bass("TRN2", target_bir_lowering=False)
        self.C = Chain()
        self.es = ExitStack()
        self.name = name
        self.in_names = []
        self.out_names = []

    def din(self, name, shape, dtype=F32):
        self.in_names.append(name)
        return self.nc.dram_tensor(name, list(shape), dtype, kind="ExternalInput").ap()

    def dout(self, name, shape, dtype=F32):
        self.out_names.append(name)
        return self.nc.dram_tensor(name, list(shape), dtype, kind="ExternalOutput").ap()

    def sb(self, shape, dtype=F32):
        self._n = getattr(self, '_n', 0) + 1
        return self.es.enter_context(self.nc.sbuf_tensor("sb%d" % self._n, list(shape), dtype))

    def ps(self, shape, dtype=F32):
        self._n = getattr(self, '_n', 0) + 1
        return self.es.enter_context(self.nc.psum_tensor("ps%d" % self._n, list(shape), dtype))

    def common_consts(self):
        self.ident_d = self.din("c_ident", (128, 128), F32)
        self.identb_d = self.din("c_identb", (128, 128), BF16)
        self.ones_d = self.din("c_ones", (1, 512), F32)
        self.onescol_d = self.din("c_onescol", (128, 2), F32)
        self.cpool_d = self.din("c_cpool", (128, 8), F32)
        self.ident = self.sb((128, 128), F32)
        self.identb = self.sb((128, 128), BF16)
        self.ones = self.sb((1, 512), F32)
        self.onescol = self.sb((128, 2), F32)
        self.cp = self.sb((128, 8), F32)
        self.C.dma(self.ident[:], self.ident_d[:])
        self.C.dma(self.identb[:], self.identb_d[:])
        self.C.dma(self.ones[:], self.ones_d[:])
        self.C.dma(self.onescol[:], self.onescol_d[:])
        self.C.dma(self.cp[:], self.cpool_d[:])

    def finish(self):
        S = self.es.enter_context(self.nc.semaphore("S"))
        blk = self.es.enter_context(self.nc.Block())
        self.C.build(blk, S)
        self.es.close()
        return self.nc


CP_NEG30 = 0
CP_EPS5 = 1
CP_ONE = 2
CP_EPS12 = 3


def emit_ln(l, tiles, ntiles, width=1024):
    C = l.C
    if not hasattr(l, '_lnb'):
        l._lnb = (l.sb((128, 1), F32), l.sb((128, 1), F32), l.sb((128, width), F32), l.sb((128, 1), F32))
    m1, m2, sq, rstd = l._lnb
    inv = 1.0 / width
    for i in range(ntiles):
        t = tiles[i]
        C.op('vector', lambda e, t=t: e.tensor_reduce(out=m1[:], in_=t[:], axis=AX.X, op=OP.add))
        C.op('scalar', lambda e, t=t: e.activation(out=sq[:], in_=t[:], func=AF.Square))
        C.op('vector', lambda e: e.tensor_reduce(out=m2[:], in_=sq[:], axis=AX.X, op=OP.add))
        C.op('vector', lambda e: e.tensor_scalar(out=m1[:], in0=m1[:], scalar1=inv, scalar2=None, op0=OP.mult))
        C.op('vector', lambda e: e.tensor_scalar(out=m2[:], in0=m2[:], scalar1=inv, scalar2=None, op0=OP.mult))
        C.op('vector', lambda e: e.tensor_scalar(out=rstd[:], in0=m1[:], scalar1=m1[:], scalar2=None, op0=OP.mult))
        C.op('vector', lambda e: e.tensor_tensor(out=m2[:], in0=m2[:], in1=rstd[:], op=OP.subtract))
        C.op('scalar', lambda e: e.activation(out=m2[:], in_=m2[:], func=AF.Ln, bias=l.cp[:, CP_EPS5:CP_EPS5 + 1]))
        C.op('scalar', lambda e: e.activation(out=rstd[:], in_=m2[:], func=AF.Exp, scale=-0.5))
        C.op('vector', lambda e, t=t: e.tensor_scalar(out=t[:], in0=t[:], scalar1=m1[:], scalar2=None, op0=OP.subtract))
        C.op('vector', lambda e, t=t: e.tensor_scalar(out=t[:], in0=t[:], scalar1=rstd[:], scalar2=None, op0=OP.mult))


def emit_transposeT(l, src_tiles, dstT, ntt, psT, nf=8):
    C = l.C
    for tt in range(ntt):
        for f in range(nf):
            C.op('tensor', lambda e, tt=tt, f=f: e.transpose(psT[:], src_tiles[tt][:, f * 128:(f + 1) * 128], l.ident[:]))
            C.op('vector', lambda e, tt=tt, f=f: e.tensor_copy(dstT[f][:, tt * 128:(tt + 1) * 128], psT[:]))


def load_w32(l, dst_tiles, dram, k, width):
    for i in range(k):
        l.C.dma(dst_tiles[i][:], dram[i * 128:(i + 1) * 128, :])


def build_A():
    l = L("A")
    C = l.C
    xin = l.din("xin", (TPC, D))
    wqT = l.din("wqT", (D, D)); bq = l.din("bq", (1, D))
    wkT = l.din("wkT", (D, D)); bk = l.din("bk", (1, D))
    wvT = l.din("wvT", (D, D)); bv = l.din("bv", (1, D))
    w1 = l.din("w1", (D, 512)); b1 = l.din("b1", (1, 512))
    w2 = l.din("w2", (512, 8)); b2 = l.din("b2", (1, 8))
    expW = l.din("expW", (8, D, D))
    expb = l.din("expb", (8, D))
    qTo = l.dout("qTo", (D, TPC))
    kTo = l.dout("kTo", (D, TPC))
    vo = l.dout("vo", (TPC, D))
    g0o = l.dout("g0o", (TPC, D))
    l.common_consts()

    xt = [l.sb((128, D)) for _ in range(NTT)]
    xnT = [l.sb((128, TPC)) for _ in range(8)]
    wbuf = [l.sb((128, D)) for _ in range(8)]
    brow = l.sb((1, D))
    psT = l.ps((128, 128))
    ps1 = l.ps((128, 512))
    stage = l.sb((128, 512))

    for tt in range(NTT):
        C.dma(xt[tt][:], xin[tt * 128:(tt + 1) * 128, :])
    emit_ln(l, xt, NTT)
    emit_transposeT(l, xt, xnT, NTT, psT)

    def proj_feat(wT_d, b_d, out_d, act=None):
        load_w32(l, wbuf, wT_d, 8, D)
        C.dma(brow[:], b_d[:])
        for m in range(8):
            for nch in range(4):
                n0 = nch * 512
                for k in range(8):
                    C.op('tensor', lambda e, m=m, n0=n0, k=k: e.matmul(
                        ps1[:], lhsT=wbuf[k][:, m * 128:(m + 1) * 128],
                        rhs=xnT[k][:, n0:n0 + 512], start=(k == 0), stop=False))
                C.op('tensor', lambda e, m=m, n0=n0: e.matmul(
                    ps1[:], lhsT=brow[:, m * 128:(m + 1) * 128],
                    rhs=l.ones[:, 0:512], start=False, stop=True))
                if act is None:
                    C.op('vector', lambda e: e.tensor_copy(stage[:], ps1[:]))
                else:
                    C.op('scalar', lambda e, act=act: e.activation(out=stage[:], in_=ps1[:], func=act))
                C.dma(out_d[m * 128:(m + 1) * 128, n0:n0 + 512], stage[:])
    proj_feat(wqT, bq, qTo)
    proj_feat(wkT, bk, kTo)

    load_w32(l, wbuf, wvT, 8, D)
    C.dma(brow[:], bv[:])
    for tt in range(NTT):
        for nch in range(2):
            n0 = nch * 512
            for k in range(8):
                C.op('tensor', lambda e, tt=tt, n0=n0, k=k: e.matmul(
                    ps1[:], lhsT=xnT[k][:, tt * 128:(tt + 1) * 128],
                    rhs=wbuf[k][:, n0:n0 + 512], start=(k == 0), stop=False))
            C.op('tensor', lambda e, n0=n0: e.matmul(
                ps1[:], lhsT=l.ones[:, 0:128], rhs=brow[:, n0:n0 + 512], start=False, stop=True))
            C.op('vector', lambda e: e.tensor_copy(stage[:], ps1[:]))
            C.dma(vo[tt * 128:(tt + 1) * 128, n0:n0 + 512], stage[:])

    h4 = [l.sb((128, TPC), BF16) for _ in range(4)]
    w1b = [wbuf[k][:, 0:512] for k in range(8)]
    for k in range(8):
        C.dma(wbuf[k][:, 0:512], w1[k * 128:(k + 1) * 128, :])
    b1row = l.sb((1, 512))
    C.dma(b1row[:], b1[:])
    for m in range(4):
        for nch in range(4):
            n0 = nch * 512
            for k in range(8):
                C.op('tensor', lambda e, m=m, n0=n0, k=k: e.matmul(
                    ps1[:], lhsT=w1b[k][:, m * 128:(m + 1) * 128],
                    rhs=xnT[k][:, n0:n0 + 512], start=(k == 0), stop=False))
            C.op('tensor', lambda e, m=m, n0=n0: e.matmul(
                ps1[:], lhsT=b1row[:, m * 128:(m + 1) * 128], rhs=l.ones[:, 0:512],
                start=False, stop=True))
            C.op('scalar', lambda e, m=m, n0=n0: e.activation(out=h4[m][:, n0:n0 + 512], in_=ps1[:], func=AF.Relu))
    w2b32 = [l.sb((128, 8)) for _ in range(4)]
    load_w32(l, w2b32, w2, 4, 8)
    w2b = [l.sb((128, 8), BF16) for _ in range(4)]
    for k in range(4):
        C.op('vector', lambda e, k=k: e.tensor_copy(w2b[k][:], w2b32[k][:]))
    b2col = l.sb((8, 1))
    C.dma(b2col[:], b2[:].rearrange("a b -> b a"), slow=True)
    lgT = l.sb((8, TPC))
    ps8 = l.ps((8, 512))
    for nch in range(4):
        n0 = nch * 512
        for k in range(4):
            C.op('tensor', lambda e, n0=n0, k=k: e.matmul(
                ps8[:], lhsT=w2b[k][:], rhs=h4[k][:, n0:n0 + 512], start=(k == 0), stop=(k == 3)))
        C.op('vector', lambda e, n0=n0: e.tensor_scalar(out=lgT[:, n0:n0 + 512], in0=ps8[:], scalar1=b2col[:], scalar2=None, op0=OP.add))
    masks = [l.sb((128, 8)) for _ in range(NTT)]
    lgtok = l.sb((128, 8))
    mx = l.sb((128, 1))
    psL = l.ps((128, 8))
    for tt in range(NTT):
        C.op('tensor', lambda e, tt=tt: e.transpose(psL[:], lgT[:, tt * 128:(tt + 1) * 128], l.ident[0:8, 0:8]))
        C.op('vector', lambda e: e.tensor_copy(lgtok[:], psL[:]))
        C.op('vector', lambda e: e.tensor_reduce(out=mx[:], in_=lgtok[:], axis=AX.X, op=OP.max))
        C.op('vector', lambda e, tt=tt: e.tensor_scalar(out=masks[tt][:], in0=lgtok[:], scalar1=mx[:], scalar2=None, op0=OP.is_ge))
    for tt in range(NTT):
        C.op('vector', lambda e, tt=tt: e.memset(xt[tt][:], 0.0))
    ebrow = l.sb((1, D))
    for ex in range(8):
        load_w32(l, wbuf, expW[ex], 8, D)
        C.dma(ebrow[:], expb[ex:ex + 1, :])
        for tt in range(NTT):
            for nch in range(2):
                n0 = nch * 512
                for k in range(8):
                    C.op('tensor', lambda e, tt=tt, n0=n0, k=k: e.matmul(
                        ps1[:], lhsT=xnT[k][:, tt * 128:(tt + 1) * 128],
                        rhs=wbuf[k][:, n0:n0 + 512], start=(k == 0), stop=False))
                C.op('tensor', lambda e, n0=n0: e.matmul(
                    ps1[:], lhsT=l.ones[:, 0:128], rhs=ebrow[:, n0:n0 + 512], start=False, stop=True))
                C.op('vector', lambda e, tt=tt, ex=ex, n0=n0: e.scalar_tensor_tensor(
                    out=xt[tt][:, n0:n0 + 512], in0=ps1[:], scalar=masks[tt][:, ex:ex + 1],
                    in1=xt[tt][:, n0:n0 + 512], op0=OP.mult, op1=OP.add))
    for tt in range(NTT):
        C.op('scalar', lambda e, tt=tt: e.activation(out=xt[tt][:], in_=xt[tt][:], func=AF.Silu))
        C.dma(g0o[tt * 128:(tt + 1) * 128, :], xt[tt][:])
    return l


def build_B():
    l = L("B")
    C = l.C
    qT4 = l.din("qT4", (256, SEQ))
    kT4 = l.din("kT4", (256, SEQ))
    v4 = l.din("v4", (SEQ, 256))
    masks_d = l.din("l0masks", (4, 128, 512))
    s0o = l.dout("s0o", (SEQ, 256))
    l.common_consts()

    khT = l.sb((64, SEQ))
    qh = l.sb((64, 512))
    vh = l.sb((128, 65 * 64))
    pT = l.sb((128, 512))
    mk = [l.sb((128, 512)) for _ in range(4)]
    for r in range(4):
        C.dma(mk[r][:], masks_d[r, :, :])
    psS = l.ps((128, 512))
    psPV = [l.ps((128, 65)) for _ in range(4)]
    rec = l.sb((128, 1))
    ost = l.sb((128, 64))

    for h in range(4):
        C.dma(khT[:], kT4[h * 64:(h + 1) * 64, :])
        C.op('vector', lambda e: e.memset(vh[:], 1.0))
        for kt in range(64):
            C.dma(vh[:, kt * 65:kt * 65 + 64], v4[kt * 128:(kt + 1) * 128, h * 64:(h + 1) * 64])
        for qc in range(16):
            q0 = qc * 512
            C.dma(qh[:], qT4[h * 64:(h + 1) * 64, q0:q0 + 512])
            nkt = qc * 4 + 4
            for kt in range(nkt):
                s0 = kt * 128
                C.op('tensor', lambda e, s0=s0: e.matmul(
                    psS[:], lhsT=khT[:, s0:s0 + 128], rhs=qh[:], start=True, stop=True))
                if s0 >= q0:
                    r = (s0 - q0) // 128
                    C.op('vector', lambda e, r=r: e.tensor_tensor(out=psS[:], in0=psS[:], in1=mk[r][:], op=OP.add))
                C.op('scalar', lambda e: e.activation(out=pT[:], in_=psS[:], func=AF.Exp,
                                                      scale=0.125, bias=l.cp[:, CP_NEG30:CP_NEG30 + 1]))
                for tc in range(4):
                    C.op('tensor', lambda e, tc=tc, kt=kt, nkt=nkt: e.matmul(
                        psPV[tc][:], lhsT=pT[:, tc * 128:(tc + 1) * 128],
                        rhs=vh[:, kt * 65:(kt + 1) * 65], start=(kt == 0), stop=(kt == nkt - 1)))
            for tc in range(4):
                C.op('vector', lambda e, tc=tc: e.tensor_scalar(out=rec[:], in0=psPV[tc][:, 64:65], scalar1=1e-25, scalar2=None, op0=OP.max))
                C.op('vector', lambda e: e.reciprocal(out=rec[:], in_=rec[:]))
                C.op('vector', lambda e, tc=tc: e.tensor_scalar(
                    out=ost[:], in0=psPV[tc][:, 0:64], scalar1=rec[:], scalar2=None, op0=OP.mult))
                C.dma(s0o[q0 + tc * 128:q0 + (tc + 1) * 128, h * 64:(h + 1) * 64], ost[:])
    return l


def emit_ffn_int8(l, xnT, wgT_d, wuT_d, wdT_d, out_d, gT, wdbuf, ps1):
    C = l.C
    wstg_i = l.sb((128, 128), I32)
    wm = l.sb((128, 128))
    wdi = l.sb((128, 512), I32)
    ustg = l.sb((128, 512))

    def mm_deq(w_d, m, cb):
        for k in range(8):
            C.dma(wstg_i[:], w_d[k * 128:(k + 1) * 128, m * 128:(m + 1) * 128])
            C.op('vector', lambda e: e.tensor_copy(wm[:], wstg_i[:]))
            C.op('scalar', lambda e: e.activation(out=wm[:], in_=wm[:], func=AF.Copy, scale=1.0 / 127.0))
            C.op('tensor', lambda e, k=k, cb=cb: e.matmul(
                ps1[:], lhsT=wm[:], rhs=xnT[k][:, cb:cb + 512], start=(k == 0), stop=(k == 7)))

    for ch in range(4):
        cb = ch * 512
        for m in range(24):
            mm_deq(wgT_d, m, cb)
            C.op('scalar', lambda e, m=m: e.activation(out=gT[m][:], in_=ps1[:], func=AF.Silu))
        for m in range(24):
            mm_deq(wuT_d, m, cb)
            C.op('vector', lambda e: e.tensor_copy(ustg[:], ps1[:]))
            C.op('vector', lambda e, m=m: e.tensor_tensor(out=gT[m][:], in0=gT[m][:], in1=ustg[:], op=OP.mult))
        for nch in range(2):
            n0 = nch * 512
            for k in range(24):
                C.dma(wdi[:], wdT_d[k * 128:(k + 1) * 128, n0:n0 + 512])
                C.op('vector', lambda e, k=k: e.tensor_copy(wdbuf[k][:], wdi[:]))
                C.op('scalar', lambda e, k=k: e.activation(out=wdbuf[k][:], in_=wdbuf[k][:], func=AF.Copy, scale=1.0 / 127.0))
            for tl in range(4):
                tt = ch * 4 + tl
                for k in range(24):
                    C.op('tensor', lambda e, tl=tl, k=k: e.matmul(
                        ps1[:], lhsT=gT[k][:, tl * 128:(tl + 1) * 128], rhs=wdbuf[k][:],
                        start=(k == 0), stop=(k == 23)))
                C.op('scalar', lambda e: e.activation(out=ustg[:], in_=ps1[:], func=AF.Silu))
                C.dma(out_d[tt * 128:(tt + 1) * 128, n0:n0 + 512], ustg[:])


def build_C():
    l = L("C")
    C = l.C
    s0 = l.din("s0", (TPC, D))
    xin = l.din("xin", (TPC, D))
    g0 = l.din("g0", (TPC, D))
    woT = l.din("woT", (D, D)); bo = l.din("bo", (1, D))
    wqT = l.din("wqT", (D, D)); bq = l.din("bq", (1, D))
    wkT = l.din("wkT", (D, D)); bk = l.din("bk", (1, D))
    wvT = l.din("wvT", (D, D)); bv = l.din("bv", (1, D))
    wgT = l.din("wgT", (D, D)); bg = l.din("bg", (1, D))
    costab = l.din("costab", (128, TPC))
    sintab = l.din("sintab", (128, TPC))
    pmrot = l.din("pmrot", (128, 128))
    fwg = l.din("fwg", (D, 3072), I32)
    fwu = l.din("fwu", (D, 3072), I32)
    fwd = l.din("fwd", (3072, D), I32)
    x1o = l.dout("x1o", (TPC, D))
    q1o = l.dout("q1o", (D, TPC))
    k1o = l.dout("k1o", (D, TPC))
    v1o = l.dout("v1o", (TPC, D))
    gate1o = l.dout("gate1o", (TPC, D))
    g1o = l.dout("g1o", (TPC, D))
    l.common_consts()

    x1 = [l.sb((128, D)) for _ in range(NTT)]
    aT = [l.sb((128, TPC)) for _ in range(8)]
    wbuf = [l.sb((128, D)) for _ in range(8)]
    brow = l.sb((1, D))
    psT = l.ps((128, 128))
    ps1 = l.ps((128, 512))
    ps2 = l.ps((128, 512))
    stage = l.sb((128, 512))
    cossb = l.sb((128, TPC))
    sinsb = l.sb((128, TPC))
    pmsb = l.sb((128, 128))

    for tt in range(NTT):
        C.dma(x1[tt][:], s0[tt * 128:(tt + 1) * 128, :])
    emit_transposeT(l, x1, aT, NTT, psT)
    load_w32(l, wbuf, woT, 8, D)
    C.dma(brow[:], bo[:])
    gstg = l.sb((128, D))
    xstg = l.sb((128, D))
    stg2 = gstg[:, 0:512]
    stg3 = gstg[:, 512:1024]
    for tt in range(NTT):
        C.dma(gstg[:], g0[tt * 128:(tt + 1) * 128, :])
        C.dma(xstg[:], xin[tt * 128:(tt + 1) * 128, :])
        for nch in range(2):
            n0 = nch * 512
            for k in range(8):
                C.op('tensor', lambda e, tt=tt, n0=n0, k=k: e.matmul(
                    ps1[:], lhsT=aT[k][:, tt * 128:(tt + 1) * 128],
                    rhs=wbuf[k][:, n0:n0 + 512], start=(k == 0), stop=False))
            C.op('tensor', lambda e, n0=n0: e.matmul(
                ps1[:], lhsT=l.ones[:, 0:128], rhs=brow[:, n0:n0 + 512], start=False, stop=True))
            C.op('vector', lambda e, tt=tt, n0=n0: e.tensor_tensor(
                out=x1[tt][:, n0:n0 + 512], in0=ps1[:], in1=gstg[:, n0:n0 + 512], op=OP.mult))
        C.op('vector', lambda e, tt=tt: e.tensor_tensor(out=x1[tt][:], in0=x1[tt][:], in1=xstg[:], op=OP.add))
        C.dma(x1o[tt * 128:(tt + 1) * 128, :], x1[tt][:])
    emit_ln(l, x1, NTT)
    emit_transposeT(l, x1, aT, NTT, psT)

    C.dma(cossb[:], costab[:])
    C.dma(sinsb[:], sintab[:])
    C.dma(pmsb[:], pmrot[:])

    def proj_rot(wT_d, b_d, out_d):
        load_w32(l, wbuf, wT_d, 8, D)
        C.dma(brow[:], b_d[:])
        for m in range(8):
            for nch in range(4):
                n0 = nch * 512
                for k in range(8):
                    C.op('tensor', lambda e, m=m, n0=n0, k=k: e.matmul(
                        ps1[:], lhsT=wbuf[k][:, m * 128:(m + 1) * 128],
                        rhs=aT[k][:, n0:n0 + 512], start=(k == 0), stop=False))
                C.op('tensor', lambda e, m=m: e.matmul(
                    ps1[:], lhsT=brow[:, m * 128:(m + 1) * 128], rhs=l.ones[:, 0:512],
                    start=False, stop=True))
                C.op('vector', lambda e: e.tensor_copy(stage[:], ps1[:]))
                C.op('tensor', lambda e: e.matmul(ps2[:], lhsT=pmsb[:], rhs=stage[:], start=True, stop=True))
                C.op('vector', lambda e, n0=n0: e.tensor_tensor(out=stg2, in0=stage[:], in1=cossb[:, n0:n0 + 512], op=OP.mult))
                C.op('vector', lambda e, n0=n0: e.tensor_tensor(out=stg3, in0=ps2[:], in1=sinsb[:, n0:n0 + 512], op=OP.mult))
                C.op('vector', lambda e: e.tensor_tensor(out=stage[:], in0=stg2, in1=stg3, op=OP.add))
                C.dma(out_d[m * 128:(m + 1) * 128, n0:n0 + 512], stage[:])
    proj_rot(wqT, bq, q1o)
    proj_rot(wkT, bk, k1o)

    def proj_tok(wT_d, b_d, out_d, act=None):
        load_w32(l, wbuf, wT_d, 8, D)
        C.dma(brow[:], b_d[:])
        for tt in range(NTT):
            for nch in range(2):
                n0 = nch * 512
                for k in range(8):
                    C.op('tensor', lambda e, tt=tt, n0=n0, k=k: e.matmul(
                        ps1[:], lhsT=aT[k][:, tt * 128:(tt + 1) * 128],
                        rhs=wbuf[k][:, n0:n0 + 512], start=(k == 0), stop=False))
                C.op('tensor', lambda e, n0=n0: e.matmul(
                    ps1[:], lhsT=l.ones[:, 0:128], rhs=brow[:, n0:n0 + 512], start=False, stop=True))
                if act is None:
                    C.op('vector', lambda e: e.tensor_copy(stage[:], ps1[:]))
                else:
                    C.op('scalar', lambda e, act=act: e.activation(out=stage[:], in_=ps1[:], func=act))
                C.dma(out_d[tt * 128:(tt + 1) * 128, n0:n0 + 512], stage[:])
    proj_tok(wvT, bv, v1o)
    proj_tok(wgT, bg, gate1o, act=AF.Silu)

    gT = [x1[i][:, j * 512:(j + 1) * 512] for i in range(12) for j in range(2)]
    wdbuf = ([wbuf[i][:, j * 512:(j + 1) * 512] for i in range(8) for j in range(2)]
             + [cossb[:, j * 512:(j + 1) * 512] for j in range(4)]
             + [sinsb[:, j * 512:(j + 1) * 512] for j in range(4)])
    emit_ffn_int8(l, aT, fwg, fwu, fwd, g1o, gT, wdbuf, ps1)
    return l


def build_D():
    l = L("D")
    C = l.C
    qT4 = l.din("qT4", (256, SEQ))
    kT4 = l.din("kT4", (256, SEQ))
    v4 = l.din("v4", (SEQ, 256))
    gam = l.din("gam", (1, 16))
    masks_d = l.din("l1masks", (4, 128, 512))
    iota2d_d = l.din("iota2d", (128, 64))
    gam4_d = l.din("gam4", (1, 4))
    s1o = l.dout("s1o", (SEQ, 256))
    l.common_consts()

    khT = l.sb((64, SEQ))
    qh = l.sb((64, 512))
    qhs = l.sb((64, 512))
    vh = l.sb((128, 65 * 64))
    pT = l.sb((128, 512))
    mk = [l.sb((128, 512)) for _ in range(4)]
    dmat = [l.sb((128, 512)) for _ in range(4)]
    acoef = [l.sb((128, 1)) for _ in range(4)]
    iota2d = l.sb((128, 64))
    ball = l.sb((128, 64))
    gcol = l.sb((128, 1))
    g4 = l.sb((1, 4))
    tmp1 = l.sb((128, 512))
    osb = l.sb((128, 64))
    ost = l.sb((128, 64))
    psS = l.ps((128, 512))
    psP = l.ps((128, 256))
    psO = l.ps((128, 256))
    psC = l.ps((128, 1))
    for r in range(4):
        C.dma(mk[r][:], masks_d[r, :, :])
    C.dma(iota2d[:], iota2d_d[:])
    C.dma(g4[:], gam4_d[:])

    for h in range(4):
        C.dma(khT[:], kT4[h * 64:(h + 1) * 64, :])
        C.op('vector', lambda e: e.memset(vh[:], 1.0))
        for kt in range(64):
            C.dma(vh[:, kt * 65:kt * 65 + 64], v4[kt * 128:(kt + 1) * 128, h * 64:(h + 1) * 64])
        C.op('tensor', lambda e, h=h: e.matmul(psC[:], lhsT=l.ones[:, 0:128], rhs=g4[:, h:h + 1], start=True, stop=True))
        C.op('scalar', lambda e: e.activation(out=gcol[:], in_=psC[:], func=AF.Sigmoid))
        for r in range(4):
            C.op('vector', lambda e, r=r: e.tensor_scalar(out=dmat[r][:], in0=mk[r][:], scalar1=gcol[:], scalar2=None, op0=OP.mult))
            C.op('scalar', lambda e, r=r: e.activation(out=dmat[r][:], in_=dmat[r][:], func=AF.Exp, scale=-1.0))
        for tc in range(4):
            C.op('vector', lambda e, tc=tc: e.tensor_scalar(out=acoef[tc][:], in0=iota2d[:, tc:tc + 1], scalar1=gcol[:], scalar2=None, op0=OP.mult))
            C.op('scalar', lambda e, tc=tc: e.activation(out=acoef[tc][:], in_=acoef[tc][:], func=AF.Exp, scale=-1.0))
        for qc in range(16):
            q0 = qc * 512
            C.dma(qh[:], qT4[h * 64:(h + 1) * 64, q0:q0 + 512])
            C.op('scalar', lambda e: e.activation(out=qhs[:], in_=qh[:], func=AF.Copy, scale=0.125))
            if q0 > 0:
                C.op('vector', lambda e, q0=q0: e.tensor_scalar(out=ball[:], in0=iota2d[:], scalar1=float(-q0), scalar2=None, op0=OP.add))
                C.op('vector', lambda e: e.tensor_scalar(out=ball[:], in0=ball[:], scalar1=gcol[:], scalar2=None, op0=OP.mult))
                C.op('scalar', lambda e: e.activation(out=ball[:], in_=ball[:], func=AF.Exp))
            npast = q0 // 128
            for kt in range(64):
                s0 = kt * 128
                C.op('tensor', lambda e, s0=s0: e.matmul(psS[:], lhsT=khT[:, s0:s0 + 128], rhs=qhs[:], start=True, stop=True))
                if s0 <= q0 - 128:
                    C.op('vector', lambda e, kt=kt: e.tensor_scalar(out=pT[:], in0=psS[:], scalar1=ball[:, kt:kt + 1], scalar2=None, op0=OP.mult))
                    dst, first, last = psP, (kt == 0), (kt == npast - 1)
                elif s0 < q0 + 512:
                    r = (s0 - q0) // 128
                    C.op('vector', lambda e, r=r: e.tensor_tensor(out=pT[:], in0=psS[:], in1=dmat[r][:], op=OP.mult))
                    dst, first, last = psO, (kt == npast), (kt == 63)
                else:
                    C.op('vector', lambda e: e.tensor_copy(pT[:], psS[:]))
                    dst, first, last = psO, False, (kt == 63)
                for tc in range(4):
                    C.op('tensor', lambda e, dst=dst, tc=tc, kt=kt, first=first, last=last: e.matmul(
                        dst[:, tc * 64:(tc + 1) * 64], lhsT=pT[:, tc * 128:(tc + 1) * 128],
                        rhs=vh[:, kt * 65:kt * 65 + 64], start=first, stop=last))
            for tc in range(4):
                if npast > 0:
                    C.op('vector', lambda e, tc=tc: e.tensor_copy(osb[:], psO[:, tc * 64:(tc + 1) * 64]))
                    C.op('vector', lambda e, tc=tc: e.scalar_tensor_tensor(
                        out=ost[:], in0=psP[:, tc * 64:(tc + 1) * 64], scalar=acoef[tc][:],
                        in1=osb[:], op0=OP.mult, op1=OP.add))
                else:
                    C.op('vector', lambda e, tc=tc: e.tensor_copy(ost[:], psO[:, tc * 64:(tc + 1) * 64]))
                C.dma(s1o[q0 + tc * 128:q0 + (tc + 1) * 128, h * 64:(h + 1) * 64], ost[:])
    return l


def build_E():
    l = L("E")
    C = l.C
    x1 = l.din("x1", (TPC, D))
    s1h = l.din("s1h", (TPC, D))
    gate1 = l.din("gate1", (TPC, D))
    g1 = l.din("g1", (TPC, D))
    wq = l.din("wq", (D, D))
    wk = l.din("wk", (D, D))
    wv = l.din("wv", (D, D))
    wgm = l.din("wgm", (D, 16))
    wf1 = l.din("wf1", (D, 64))
    wf2 = l.din("wf2", (64, D))
    wog1 = l.din("wog1", (D, 64))
    wog2 = l.din("wog2", (64, D))
    fwg = l.din("fwg", (D, 3072), I32)
    fwu = l.din("fwu", (D, 3072), I32)
    fwd = l.din("fwd", (3072, D), I32)
    x2o = l.dout("x2o", (TPC, D))
    q2o = l.dout("q2o", (D, TPC))
    k2o = l.dout("k2o", (D, TPC))
    v2o = l.dout("v2o", (D, TPC))
    gk2o = l.dout("gk2o", (D, TPC))
    gam2o = l.dout("gam2o", (16, TPC))
    og2o = l.dout("og2o", (TPC, D))
    g2o = l.dout("g2o", (TPC, D))
    l.common_consts()

    xt = [l.sb((128, D)) for _ in range(NTT)]
    aT = [l.sb((128, TPC)) for _ in range(8)]
    wbuf = [l.sb((128, D)) for _ in range(8)]
    psT = l.ps((128, 128))
    ps1 = l.ps((128, 512))
    stage = l.sb((128, 512))
    t1 = l.sb((128, D))
    t2 = l.sb((128, D))

    for tt in range(NTT):
        C.dma(xt[tt][:], x1[tt * 128:(tt + 1) * 128, :])
        C.dma(t1[:], s1h[tt * 128:(tt + 1) * 128, :])
        C.dma(t2[:], gate1[tt * 128:(tt + 1) * 128, :])
        C.op('vector', lambda e: e.tensor_tensor(out=t1[:], in0=t1[:], in1=t2[:], op=OP.mult))
        C.dma(t2[:], g1[tt * 128:(tt + 1) * 128, :])
        C.op('vector', lambda e: e.tensor_tensor(out=t1[:], in0=t1[:], in1=t2[:], op=OP.mult))
        C.op('vector', lambda e, tt=tt: e.tensor_tensor(out=xt[tt][:], in0=xt[tt][:], in1=t1[:], op=OP.add))
        C.dma(x2o[tt * 128:(tt + 1) * 128, :], xt[tt][:])
    emit_ln(l, xt, NTT)
    emit_transposeT(l, xt, aT, NTT, psT)

    def proj_feat(w_d, out_d, act=None, act2=None):
        load_w32(l, wbuf, w_d, 8, D)
        for m in range(8):
            for nch in range(4):
                n0 = nch * 512
                for k in range(8):
                    C.op('tensor', lambda e, m=m, n0=n0, k=k: e.matmul(
                        ps1[:], lhsT=wbuf[k][:, m * 128:(m + 1) * 128],
                        rhs=aT[k][:, n0:n0 + 512], start=(k == 0), stop=(k == 7)))
                if act is None:
                    C.op('vector', lambda e: e.tensor_copy(stage[:], ps1[:]))
                else:
                    C.op('scalar', lambda e, act=act: e.activation(out=stage[:], in_=ps1[:], func=act))
                if act2 is not None:
                    C.op('scalar', lambda e, act2=act2: e.activation(out=stage[:], in_=stage[:], func=act2))
                C.dma(out_d[m * 128:(m + 1) * 128, n0:n0 + 512], stage[:])
    proj_feat(wq, q2o, act=AF.Silu)
    proj_feat(wk, k2o, act=AF.Silu)
    proj_feat(wv, v2o)
    f1 = l.sb((64, TPC))
    wf1b = [l.sb((128, 64)) for _ in range(8)]
    load_w32(l, wf1b, wf1, 8, 64)
    ps64 = l.ps((64, 512))
    for nch in range(4):
        n0 = nch * 512
        for k in range(8):
            C.op('tensor', lambda e, n0=n0, k=k: e.matmul(
                ps64[:], lhsT=wf1b[k][:], rhs=aT[k][:, n0:n0 + 512], start=(k == 0), stop=(k == 7)))
        C.op('vector', lambda e, n0=n0: e.tensor_copy(f1[:, n0:n0 + 512], ps64[:]))
    wf2b = l.sb((64, D))
    C.dma(wf2b[:], wf2[:])
    for m in range(8):
        for nch in range(4):
            n0 = nch * 512
            C.op('tensor', lambda e, m=m, n0=n0: e.matmul(
                ps1[:], lhsT=wf2b[:, m * 128:(m + 1) * 128], rhs=f1[:, n0:n0 + 512], start=True, stop=True))
            C.op('scalar', lambda e: e.activation(out=stage[:], in_=ps1[:], func=AF.Sigmoid))
            C.op('scalar', lambda e: e.activation(out=stage[:], in_=stage[:], func=AF.Ln))
            C.dma(gk2o[m * 128:(m + 1) * 128, n0:n0 + 512], stage[:])
    wgmb = [l.sb((128, 16)) for _ in range(8)]
    load_w32(l, wgmb, wgm, 8, 16)
    ps16 = l.ps((16, 512))
    st16 = stage[0:16, 0:512]
    for nch in range(4):
        n0 = nch * 512
        for k in range(8):
            C.op('tensor', lambda e, n0=n0, k=k: e.matmul(
                ps16[:], lhsT=wgmb[k][:], rhs=aT[k][:, n0:n0 + 512], start=(k == 0), stop=(k == 7)))
        C.op('scalar', lambda e: e.activation(out=st16, in_=ps16[:], func=AF.Sigmoid))
        C.op('vector', lambda e: e.tensor_scalar(out=st16, in0=st16, scalar1=-1.0, scalar2=None, op0=OP.mult))
        C.dma(gam2o[:, n0:n0 + 512], st16)
    wo1b = [l.sb((128, 64)) for _ in range(8)]
    load_w32(l, wo1b, wog1, 8, 64)
    for nch in range(4):
        n0 = nch * 512
        for k in range(8):
            C.op('tensor', lambda e, n0=n0, k=k: e.matmul(
                ps64[:], lhsT=wo1b[k][:], rhs=aT[k][:, n0:n0 + 512], start=(k == 0), stop=(k == 7)))
        C.op('vector', lambda e, n0=n0: e.tensor_copy(f1[:, n0:n0 + 512], ps64[:]))
    wo2b = l.sb((64, D))
    C.dma(wo2b[:], wog2[:])
    for tt in range(NTT):
        for nch in range(2):
            n0 = nch * 512
            C.op('tensor', lambda e, tt=tt, n0=n0: e.matmul(
                ps1[:], lhsT=f1[:, tt * 128:(tt + 1) * 128], rhs=wo2b[:, n0:n0 + 512], start=True, stop=True))
            C.op('scalar', lambda e: e.activation(out=stage[:], in_=ps1[:], func=AF.Sigmoid))
            C.dma(og2o[tt * 128:(tt + 1) * 128, n0:n0 + 512], stage[:])

    gT = [xt[i][:, j * 512:(j + 1) * 512] for i in range(12) for j in range(2)]
    wdbuf = ([wbuf[i][:, j * 512:(j + 1) * 512] for i in range(8) for j in range(2)]
             + [xt[12 + i][:, j * 512:(j + 1) * 512] for i in range(4) for j in range(2)])
    emit_ffn_int8(l, aT, fwg, fwu, fwd, g2o, gT, wdbuf, ps1)
    return l


def build_F():
    l = L("F")
    C = l.C
    qh4 = l.din("qh4", (256, SEQ))
    kh4 = l.din("kh4", (256, SEQ))
    vh4 = l.din("vh4", (256, SEQ))
    gkh4 = l.din("gkh4", (256, SEQ))
    gam4 = l.din("gam4", (4, SEQ))
    cuml_d = l.din("cuml", (128, 128))
    maskls_d = l.din("maskls", (128, 128))
    o2o = l.dout("o2o", (SEQ, 256))
    l.common_consts()
    W = 64

    qT = l.sb((64, W)); kT = l.sb((64, W)); vT = l.sb((64, W)); gkT = l.sb((64, W))
    grow = l.sb((1, W))
    cuml = l.sb((128, 128)); maskls = l.sb((128, 128))
    C.dma(cuml[:], cuml_d[:])
    C.dma(maskls[:], maskls_d[:])
    Sf = l.sb((64, 64)); Stf = l.sb((64, 64)); Stb = l.sb((64, 64))
    gcol = l.sb((W, 1))
    gk_tp = l.sb((W, 64)); cum_tp = l.sb((W, 64)); cum_f = l.sb((64, W))
    nm = l.sb((64, 1)); ecm = l.sb((64, 1))
    ecum_f = l.sb((64, W)); einv_f = l.sb((64, W)); e_f = l.sb((64, W))
    kf = l.sb((64, W)); qf = l.sb((64, W))
    adf = l.sb((64, W)); ktf = l.sb((64, W)); qhat = l.sb((64, W))
    kt_tp = l.sb((W, 64)); v_tp = l.sb((W, 64))
    wsb = l.sb((W, W)); zsb = l.sb((W, W))
    wnx = l.sb((W, W)); znx = l.sb((W, W))
    usb = l.sb((W, W))
    bf = l.sb((W, 64))
    of = l.sb((W, 64))
    ps1 = l.ps((128, 128))
    ps2 = l.ps((128, 128))
    ps3 = l.ps((128, 128))
    psT = l.ps((128, 128))
    psX = l.ps((128, 1))
    I64 = l.ident[0:64, 0:64]

    for h in range(4):
        C.op('vector', lambda e: e.memset(Sf[:], 0.0))
        C.op('vector', lambda e: e.memset(Stb[:], 0.0))
        for c in range(SEQ // W):
            t0 = c * W
            C.dma(qT[:], qh4[h * 64:(h + 1) * 64, t0:t0 + W])
            C.dma(kT[:], kh4[h * 64:(h + 1) * 64, t0:t0 + W])
            C.dma(vT[:], vh4[h * 64:(h + 1) * 64, t0:t0 + W])
            C.dma(gkT[:], gkh4[h * 64:(h + 1) * 64, t0:t0 + W])
            C.dma(grow[:], gam4[h:h + 1, t0:t0 + W])
            C.op('tensor', lambda e: e.matmul(psX[0:W, :], lhsT=grow[:], rhs=l.ones[:, 0:1], start=True, stop=True))
            C.op('vector', lambda e: e.tensor_copy(gcol[:], psX[0:W, :]))
            C.op('tensor', lambda e: e.transpose(psT[0:W, 0:64], gkT[:], I64))
            C.op('vector', lambda e: e.tensor_copy(gk_tp[:], psT[0:W, 0:64]))
            C.op('tensor', lambda e: e.matmul(ps1[0:W, 0:64], lhsT=cuml[0:W, 0:W], rhs=gk_tp[:], start=True, stop=True))
            C.op('vector', lambda e: e.tensor_copy(cum_tp[:], ps1[0:W, 0:64]))
            C.op('tensor', lambda e: e.transpose(psT[0:64, 0:W], cum_tp[:], I64))
            C.op('vector', lambda e: e.tensor_copy(cum_f[:], psT[0:64, 0:W]))
            C.op('vector', lambda e: e.tensor_scalar(out=nm[:], in0=cum_f[:, 31:32], scalar1=-1.0, scalar2=None, op0=OP.mult))
            C.op('scalar', lambda e: e.activation(out=ecum_f[:], in_=cum_f[:], func=AF.Exp, bias=nm[:]))
            C.op('scalar', lambda e: e.activation(out=einv_f[:], in_=cum_f[:], func=AF.Exp, scale=-1.0, bias=cum_f[:, 31:32]))
            C.op('scalar', lambda e: e.activation(out=e_f[:], in_=gkT[:], func=AF.Exp))
            C.op('scalar', lambda e: e.activation(out=ecm[:], in_=cum_f[:, 31:32], func=AF.Exp))
            C.op('vector', lambda e: e.tensor_copy(kf[:], kT[:]))
            C.op('vector', lambda e: e.tensor_copy(qf[:], qT[:]))
            C.op('vector', lambda e: e.tensor_tensor(out=adf[:], in0=kf[:], in1=e_f[:], op=OP.mult))
            C.op('vector', lambda e: e.tensor_tensor(out=adf[:], in0=adf[:], in1=ecum_f[:], op=OP.mult))
            C.op('vector', lambda e: e.tensor_tensor(out=ktf[:], in0=kf[:], in1=einv_f[:], op=OP.mult))
            C.op('vector', lambda e: e.tensor_tensor(out=qhat[:], in0=qf[:], in1=ecum_f[:], op=OP.mult))
            C.op('vector', lambda e: e.tensor_scalar(out=Stf[:], in0=Sf[:], scalar1=ecm[:], scalar2=None, op0=OP.mult))
            C.op('vector', lambda e: e.tensor_copy(Stb[:], Stf[:]))
            C.op('tensor', lambda e: e.matmul(ps1[0:W, 0:W], lhsT=adf[:], rhs=ktf[:], start=True, stop=True))
            C.op('vector', lambda e: e.scalar_tensor_tensor(out=wsb[:], in0=ps1[0:W, 0:W], scalar=gcol[:], in1=maskls[0:W, 0:W], op0=OP.mult, op1=OP.mult))
            C.op('tensor', lambda e: e.transpose(psT[0:W, 0:W], wsb[:], I64))
            C.op('vector', lambda e: e.tensor_copy(zsb[:], psT[0:W, 0:W]))
            C.op('tensor', lambda e: e.matmul(ps1[0:W, 0:W], lhsT=ktf[:], rhs=qhat[:], start=True, stop=True))
            C.op('vector', lambda e: e.tensor_tensor(out=usb[:], in0=ps1[0:W, 0:W], in1=cuml[0:W, 0:W], op=OP.mult))
            C.op('tensor', lambda e: e.transpose(psT[0:W, 0:64], vT[:], I64))
            C.op('vector', lambda e: e.tensor_copy(v_tp[:], psT[0:W, 0:64]))
            C.op('tensor', lambda e: e.matmul(ps2[0:W, 0:64], lhsT=adf[:], rhs=Stb[:], start=True, stop=True))
            C.op('vector', lambda e: e.scalar_tensor_tensor(out=bf[:], in0=ps2[0:W, 0:64], scalar=gcol[:], in1=v_tp[:], op0=OP.mult, op1=OP.add))
            for it in range(6):
                C.op('tensor', lambda e: e.matmul(ps2[0:W, 0:64], lhsT=zsb[:], rhs=bf[:], start=True, stop=True))
                C.op('vector', lambda e: e.tensor_tensor(out=bf[:], in0=bf[:], in1=ps2[0:W, 0:64], op=OP.add))
                if it < 5:
                    C.op('tensor', lambda e: e.matmul(ps2[0:W, 0:W], lhsT=zsb[:], rhs=wsb[:], start=True, stop=True))
                    C.op('vector', lambda e: e.tensor_copy(wnx[:], ps2[0:W, 0:W]))
                    C.op('tensor', lambda e: e.matmul(ps3[0:W, 0:W], lhsT=wsb[:], rhs=zsb[:], start=True, stop=True))
                    C.op('vector', lambda e: e.tensor_copy(znx[:], ps3[0:W, 0:W]))
                    C.op('vector', lambda e: e.tensor_copy(wsb[:], wnx[:]))
                    C.op('vector', lambda e: e.tensor_copy(zsb[:], znx[:]))
            C.op('tensor', lambda e: e.matmul(ps2[0:W, 0:64], lhsT=usb[:], rhs=bf[:], start=True, stop=False))
            C.op('tensor', lambda e: e.matmul(ps2[0:W, 0:64], lhsT=qhat[:], rhs=Stb[:], start=False, stop=True))
            C.op('vector', lambda e: e.tensor_copy(of[:], ps2[0:W, 0:64]))
            C.dma(o2o[t0:t0 + W, h * 64:(h + 1) * 64], of[:])
            C.op('tensor', lambda e: e.transpose(psT[0:W, 0:64], ktf[:], I64))
            C.op('vector', lambda e: e.tensor_copy(kt_tp[:], psT[0:W, 0:64]))
            C.op('tensor', lambda e: e.matmul(ps2[0:64, 0:64], lhsT=kt_tp[:], rhs=bf[:], start=True, stop=True))
            C.op('vector', lambda e: e.tensor_tensor(out=Sf[:], in0=Stf[:], in1=ps2[0:64, 0:64], op=OP.add))
            C.op('vector', lambda e: e.tensor_scalar(out=Sf[:], in0=Sf[:], scalar1=ecum_f[:, W - 1:W], scalar2=None, op0=OP.mult))
    return l


def build_G():
    l = L("G")
    C = l.C
    x2 = l.din("x2", (TPC, D))
    o2 = l.din("o2", (TPC, D))
    og2 = l.din("og2", (TPC, D))
    g2 = l.din("g2", (TPC, D))
    wo = l.din("wo", (D, D))
    normw_d = l.din("normw", (128, 8))
    ipT = l.din("ipT", (D, 4240))
    ipnat = l.din("ipnat", (4240, D))
    xnw_d = l.din("xnw", (128, 8))
    xnb_d = l.din("xnb", (128, 8))
    dtb_d = l.din("dtb", (16, 1))
    x3o = l.dout("x3o", (TPC, D))
    xn3o = l.dout("xn3o", (TPC, D))
    zTo = l.dout("zTo", (TPC, TPC))
    xbco = l.dout("xbco", (2176, TPC))
    dtTo = l.dout("dtTo", (16, TPC))
    l.common_consts()

    xt = [l.sb((128, D)) for _ in range(NTT)]
    aT = [l.sb((128, TPC)) for _ in range(8)]
    wbuf = [l.sb((128, D)) for _ in range(8)]
    psT = l.ps((128, 128))
    ps1 = l.ps((128, 512))
    stage = l.sb((128, 512))
    t1 = l.sb((128, D))
    nw = l.sb((128, 8))
    xnw = l.sb((128, 8))
    xnb = l.sb((128, 8))
    C.dma(nw[:], normw_d[:])
    C.dma(xnw[:], xnw_d[:])
    C.dma(xnb[:], xnb_d[:])

    for tt in range(NTT):
        C.dma(xt[tt][:], o2[tt * 128:(tt + 1) * 128, :])
        C.dma(t1[:], og2[tt * 128:(tt + 1) * 128, :])
        C.op('vector', lambda e, tt=tt: e.tensor_tensor(out=xt[tt][:], in0=xt[tt][:], in1=t1[:], op=OP.mult))
    emit_ln(l, xt, NTT)
    emit_transposeT(l, xt, aT, NTT, psT)
    load_w32(l, wbuf, wo, 8, D)
    for k in range(8):
        C.op('vector', lambda e, k=k: e.tensor_scalar(out=wbuf[k][:], in0=wbuf[k][:], scalar1=nw[:, k:k + 1], scalar2=None, op0=OP.mult))
    gstg = l.sb((128, D))
    for tt in range(NTT):
        C.dma(gstg[:], g2[tt * 128:(tt + 1) * 128, :])
        for nch in range(2):
            n0 = nch * 512
            for k in range(8):
                C.op('tensor', lambda e, tt=tt, n0=n0, k=k: e.matmul(
                    ps1[:], lhsT=aT[k][:, tt * 128:(tt + 1) * 128],
                    rhs=wbuf[k][:, n0:n0 + 512], start=(k == 0), stop=(k == 7)))
            C.op('vector', lambda e, tt=tt, n0=n0: e.tensor_tensor(
                out=xt[tt][:, n0:n0 + 512], in0=ps1[:], in1=gstg[:, n0:n0 + 512], op=OP.mult))
        C.dma(t1[:], x2[tt * 128:(tt + 1) * 128, :])
        C.op('vector', lambda e, tt=tt: e.tensor_tensor(out=xt[tt][:], in0=xt[tt][:], in1=t1[:], op=OP.add))
        C.dma(x3o[tt * 128:(tt + 1) * 128, :], xt[tt][:])
    emit_ln(l, xt, NTT)
    for tt in range(NTT):
        C.dma(xn3o[tt * 128:(tt + 1) * 128, :], xt[tt][:])
    emit_ln(l, xt, NTT)
    emit_transposeT(l, xt, aT, NTT, psT)

    rstg = l.sb((128, D))
    rn = [l.sb((128, 1)) for _ in range(34)]
    sqbuf = l.sb((128, D))
    for rt in range(34):
        nr = 128 if rt < 33 else 16
        C.dma(rstg[0:nr, :], ipnat[rt * 128:rt * 128 + nr, :])
        C.op('scalar', lambda e, nr=nr: e.activation(out=sqbuf[0:nr, :], in_=rstg[0:nr, :], func=AF.Square))
        C.op('vector', lambda e, rt=rt, nr=nr: e.tensor_reduce(out=rn[rt][0:nr, :], in_=sqbuf[0:nr, :], axis=AX.X, op=OP.add))
        C.op('scalar', lambda e, rt=rt, nr=nr: e.activation(out=rn[rt][0:nr, :], in_=rn[rt][0:nr, :], func=AF.Sqrt))
        C.op('vector', lambda e, rt=rt, nr=nr: e.tensor_scalar(out=rn[rt][0:nr, :], in0=rn[rt][0:nr, :], scalar1=1e-6, scalar2=None, op0=OP.max))
        C.op('vector', lambda e, rt=rt, nr=nr: e.reciprocal(out=rn[rt][0:nr, :], in_=rn[rt][0:nr, :]))
        C.op('vector', lambda e, rt=rt, nr=nr: e.tensor_scalar(out=rn[rt][0:nr, :], in0=rn[rt][0:nr, :], scalar1=float(D) ** -0.5, scalar2=None, op0=OP.mult))
    ipm = [l.sb((128, 128)) for _ in range(8)]
    cvec = [l.sb((128, 1)) for _ in range(34)]
    psc = l.ps((128, 1))
    dtbc = l.sb((16, 1))
    C.dma(dtbc[:], dtb_d[:])
    for m in range(34):
        nr = 128 if m < 33 else 16
        for k in range(8):
            C.dma(rstg[:, 0:nr], ipT[k * 128:(k + 1) * 128, m * 128:m * 128 + nr])
            C.op('tensor', lambda e, m=m, k=k, nr=nr: e.matmul(
                psc[0:nr, :], lhsT=rstg[:, 0:nr], rhs=xnb[:, k:k + 1], start=(k == 0), stop=(k == 7)))
        C.op('vector', lambda e, m=m, nr=nr: e.tensor_copy(cvec[m][0:nr, :], psc[0:nr, :]))
        C.op('vector', lambda e, m=m, nr=nr: e.tensor_scalar(out=cvec[m][0:nr, :], in0=cvec[m][0:nr, :], scalar1=rn[m][0:nr, :], scalar2=None, op0=OP.mult))
    for m in range(34):
        nr = 128 if m < 33 else 16
        for k in range(8):
            C.dma(ipm[k][:, 0:nr], ipT[k * 128:(k + 1) * 128, m * 128:m * 128 + nr])
            C.op('vector', lambda e, k=k, nr=nr: e.tensor_scalar(out=ipm[k][:, 0:nr], in0=ipm[k][:, 0:nr], scalar1=xnw[:, k:k + 1], scalar2=None, op0=OP.mult))
        for nch in range(4):
            n0 = nch * 512
            for k in range(8):
                C.op('tensor', lambda e, m=m, n0=n0, k=k, nr=nr: e.matmul(
                    ps1[0:nr, :], lhsT=ipm[k][:, 0:nr],
                    rhs=aT[k][:, n0:n0 + 512], start=(k == 0), stop=(k == 7)))
            C.op('vector', lambda e, m=m, n0=n0, nr=nr: e.tensor_scalar(
                out=stage[0:nr, :], in0=ps1[0:nr, :], scalar1=rn[m][0:nr, :], scalar2=cvec[m][0:nr, :],
                op0=OP.mult, op1=OP.add))
            r0 = m * 128
            if m < 16:
                C.dma(zTo[r0:r0 + 128, n0:n0 + 512], stage[:])
            elif m < 33:
                C.dma(xbco[r0 - 2048:r0 - 2048 + 128, n0:n0 + 512], stage[:])
            else:
                C.op('vector', lambda e: e.tensor_scalar(out=stage[0:16, :], in0=stage[0:16, :], scalar1=dtbc[:, 0:1], scalar2=None, op0=OP.add))
                C.op('scalar', lambda e: e.activation(out=stage[0:16, :], in_=stage[0:16, :], func=AF.Exp))
                C.op('scalar', lambda e: e.activation(out=stage[0:16, :], in_=stage[0:16, :], func=AF.Ln, bias=l.cp[0:16, CP_ONE:CP_ONE + 1]))
                C.dma(dtTo[:, n0:n0 + 512], stage[0:16, :])
    return l


def build_H():
    l = L("H")
    C = l.C
    xbc5 = l.din("xbc5", (640, SEQ))
    cw5 = l.din("cw5", (640, 4))
    cb5 = l.din("cb5", (640, 1))
    dt4_d = l.din("dt4", (4, SEQ))
    arow_d = l.din("arow4", (1, 4))
    drow_d = l.din("drow4", (1, 4))
    cuml_d = l.din("cuml", (128, 128))
    o_yT = l.dout("yTo", (512, SEQ))
    l.common_consts()

    cuml = l.sb((128, 128))
    C.dma(cuml[:], cuml_d[:])
    negmask = l.sb((128, 128))
    C.op('vector', lambda e: e.tensor_scalar(out=negmask[:], in0=cuml[:], scalar1=-1.0, scalar2=1e30, op0=OP.add, op1=OP.mult))
    xpad = l.sb((128, SEQ + 3))
    acc = l.sb((128, SEQ))
    xconvh = l.sb((128, SEQ))
    bcconv = l.sb((128, SEQ))
    cwt = l.sb((128, 4))
    cbt = l.sb((128, 1))
    dtc1 = l.sb((1, 128))
    arow = l.sb((1, 4))
    drow = l.sb((1, 4))
    C.dma(arow[:], arow_d[:])
    C.dma(drow[:], drow_d[:])
    acol = l.sb((128, 1)); dcol = l.sb((128, 1))
    dtcol = l.sb((128, 1)); dacol = l.sb((128, 1)); cumcol = l.sb((128, 1))
    ncum = l.sb((128, 1)); cumrow = l.sb((1, 128)); clcol = l.sb((128, 1))
    de = l.sb((128, 1)); sc2 = l.sb((128, 1)); ecumcol = l.sb((128, 1)); ecl = l.sb((128, 1))
    sc_sb = l.sb((128, 128)); seg = l.sb((128, 128)); attb = l.sb((128, 128)); ct_sb = l.sb((64, 128))
    x_tp = l.sb((128, 128)); xs = l.sb((128, 128)); b_tp = l.sb((128, 64))
    yi = l.sb((128, 128)); ysb = l.sb((128, 128))
    Sf = l.sb((64, 128))
    ps1 = l.ps((128, 128))
    ps2 = l.ps((128, 128))
    ps3 = l.ps((128, 128))
    psT = l.ps((128, 128))
    psX = l.ps((128, 1))
    psR = l.ps((1, 128))

    def conv_tile(j, dstbuf):
        C.op('vector', lambda e: e.memset(xpad[:, 0:3], 0.0))
        C.dma(xpad[:, 3:SEQ + 3], xbc5[j * 128:(j + 1) * 128, :])
        C.dma(cwt[:], cw5[j * 128:(j + 1) * 128, :], slow=True)
        C.dma(cbt[:], cb5[j * 128:(j + 1) * 128, :], slow=True)
        C.op('vector', lambda e: e.tensor_scalar(out=acc[:], in0=xpad[:, 0:SEQ], scalar1=cwt[:, 0:1], scalar2=None, op0=OP.mult))
        for t in range(1, 4):
            C.op('vector', lambda e, t=t: e.scalar_tensor_tensor(
                out=acc[:], in0=xpad[:, t:SEQ + t], scalar=cwt[:, t:t + 1], in1=acc[:], op0=OP.mult, op1=OP.add))
        C.op('vector', lambda e: e.tensor_scalar(out=acc[:], in0=acc[:], scalar1=cbt[:, 0:1], scalar2=None, op0=OP.add))
        C.op('scalar', lambda e, dstbuf=dstbuf: e.activation(out=dstbuf[:], in_=acc[:], func=AF.Silu))

    conv_tile(4, bcconv)
    for h in range(4):
        conv_tile(h, xconvh)
        C.op('tensor', lambda e, h=h: e.matmul(psX[:], lhsT=l.ones[:, 0:128], rhs=arow[0:1, h:h + 1], start=True, stop=True))
        C.op('scalar', lambda e: e.activation(out=acol[:], in_=psX[:], func=AF.Exp))
        C.op('vector', lambda e: e.tensor_scalar(out=acol[:], in0=acol[:], scalar1=-1.0, scalar2=None, op0=OP.mult))
        C.op('tensor', lambda e, h=h: e.matmul(psX[:], lhsT=l.ones[:, 0:128], rhs=drow[0:1, h:h + 1], start=True, stop=True))
        C.op('vector', lambda e: e.tensor_copy(dcol[:], psX[:]))
        C.op('vector', lambda e: e.memset(Sf[:], 0.0))
        for c in range(64):
            t0 = c * 128
            C.dma(dtc1[:], dt4_d[h:h + 1, t0:t0 + 128])
            C.op('tensor', lambda e: e.matmul(psX[:], lhsT=dtc1[:], rhs=l.ones[:, 0:1], start=True, stop=True))
            C.op('vector', lambda e: e.tensor_copy(dtcol[:], psX[:]))
            C.op('vector', lambda e: e.tensor_tensor(out=dacol[:], in0=dtcol[:], in1=acol[:], op=OP.mult))
            C.op('tensor', lambda e: e.matmul(psX[:], lhsT=cuml[:], rhs=dacol[:], start=True, stop=True))
            C.op('vector', lambda e: e.tensor_copy(cumcol[:], psX[:]))
            C.op('vector', lambda e: e.tensor_scalar(out=ncum[:], in0=cumcol[:], scalar1=-1.0, scalar2=None, op0=OP.mult))
            C.op('tensor', lambda e: e.matmul(psR[:], lhsT=cumcol[:], rhs=l.ident[:], start=True, stop=True))
            C.op('vector', lambda e: e.tensor_copy(cumrow[:], psR[:]))
            C.op('vector', lambda e, t0=t0: e.tensor_copy(ct_sb[:], bcconv[64:128, t0:t0 + 128]))
            C.op('tensor', lambda e, t0=t0: e.matmul(ps1[:], lhsT=bcconv[0:64, t0:t0 + 128], rhs=ct_sb[:], start=True, stop=True))
            C.op('vector', lambda e: e.tensor_copy(sc_sb[:], ps1[:]))
            C.op('tensor', lambda e: e.matmul(ps2[:], lhsT=l.ones[:, 0:128], rhs=cumrow[:], start=True, stop=True))
            C.op('vector', lambda e: e.tensor_scalar(out=seg[:], in0=ps2[:], scalar1=ncum[:], scalar2=None, op0=OP.add))
            C.op('vector', lambda e: e.tensor_tensor(out=seg[:], in0=seg[:], in1=negmask[:], op=OP.add))
            C.op('scalar', lambda e: e.activation(out=seg[:], in_=seg[:], func=AF.Exp))
            C.op('vector', lambda e: e.tensor_tensor(out=seg[:], in0=seg[:], in1=sc_sb[:], op=OP.mult))
            C.op('vector', lambda e: e.tensor_scalar(out=attb[:], in0=seg[:], scalar1=dtcol[:], scalar2=None, op0=OP.mult))
            C.op('tensor', lambda e, t0=t0: e.transpose(psT[:], xconvh[:, t0:t0 + 128], l.ident[:]))
            C.op('vector', lambda e: e.tensor_copy(x_tp[:], psT[:]))
            C.op('tensor', lambda e: e.matmul(ps2[:], lhsT=attb[:], rhs=x_tp[:], start=True, stop=True))
            C.op('vector', lambda e: e.tensor_copy(yi[:], ps2[:]))
            C.op('tensor', lambda e: e.matmul(ps3[:], lhsT=ct_sb[:], rhs=Sf[:], start=True, stop=True))
            C.op('scalar', lambda e: e.activation(out=ecumcol[:], in_=cumcol[:], func=AF.Exp))
            C.op('vector', lambda e: e.scalar_tensor_tensor(out=ysb[:], in0=ps3[:], scalar=ecumcol[:], in1=yi[:], op0=OP.mult, op1=OP.add))
            C.op('vector', lambda e: e.scalar_tensor_tensor(out=ysb[:], in0=x_tp[:], scalar=dcol[:], in1=ysb[:], op0=OP.mult, op1=OP.add))
            C.op('tensor', lambda e: e.matmul(psX[:], lhsT=l.ones[:, 0:128], rhs=cumrow[:, 127:128], start=True, stop=True))
            C.op('vector', lambda e: e.tensor_copy(clcol[:], psX[:]))
            C.op('vector', lambda e: e.tensor_tensor(out=de[:], in0=clcol[:], in1=cumcol[:], op=OP.subtract))
            C.op('scalar', lambda e: e.activation(out=de[:], in_=de[:], func=AF.Exp))
            C.op('vector', lambda e: e.tensor_tensor(out=sc2[:], in0=de[:], in1=dtcol[:], op=OP.mult))
            C.op('vector', lambda e: e.tensor_scalar(out=xs[:], in0=x_tp[:], scalar1=sc2[:], scalar2=None, op0=OP.mult))
            C.op('tensor', lambda e, t0=t0: e.transpose(psT[:, 0:64], bcconv[0:64, t0:t0 + 128], l.ident[0:64, 0:64]))
            C.op('vector', lambda e: e.tensor_copy(b_tp[:], psT[:, 0:64]))
            C.op('tensor', lambda e: e.matmul(ps3[0:64, :], lhsT=b_tp[:], rhs=xs[:], start=True, stop=True))
            C.op('scalar', lambda e: e.activation(out=ecl[:], in_=clcol[:], func=AF.Exp))
            C.op('vector', lambda e: e.scalar_tensor_tensor(out=Sf[:], in0=Sf[:], scalar=ecl[0:64, :], in1=ps3[0:64, :], op0=OP.mult, op1=OP.add))
            C.op('tensor', lambda e: e.transpose(psT[:], ysb[:], l.ident[:]))
            C.op('vector', lambda e: e.tensor_copy(ysb[:], psT[:]))
            C.dma(o_yT[h * 128:(h + 1) * 128, t0:t0 + 128], ysb[:])
    return l


def build_I1():
    l = L("I1")
    C = l.C
    zT = l.din("zT", (2048, TPC))
    yT = l.din("yT", (2048, TPC))
    rmsw_d = l.din("rmsw", (128, 16))
    opT = l.din("opT", (2048, D))
    opnat = l.din("opnat", (D, 2048))
    s3o = l.dout("s3o", (TPC, D))
    l.common_consts()

    gn = [l.sb((128, TPC), BF16) for _ in range(16)]
    zf = l.sb((128, TPC))
    yf = l.sb((128, TPC))
    sqb = l.sb((128, 512), BF16)
    onescol_bf = l.sb((128, 1), BF16)
    C.op('vector', lambda e: e.memset(onescol_bf[:], 1.0))
    rmw = l.sb((128, 16))
    C.dma(rmw[:], rmsw_d[:])
    for f in range(16):
        C.dma(zf[:], zT[f * 128:(f + 1) * 128, :])
        C.dma(yf[:], yT[f * 128:(f + 1) * 128, :])
        C.op('scalar', lambda e: e.activation(out=zf[:], in_=zf[:], func=AF.Silu))
        C.op('vector', lambda e, f=f: e.tensor_tensor(out=gn[f][:], in0=yf[:], in1=zf[:], op=OP.mult))
    psR = l.ps((1, 512))
    ps2 = l.ps((128, 512))
    rstd = l.sb((1, 512))
    rb_bf = l.sb((128, 512), BF16)
    for nch in range(4):
        n0 = nch * 512
        for f in range(16):
            C.op('scalar', lambda e, f=f, n0=n0: e.activation(out=sqb[:], in_=gn[f][:, n0:n0 + 512], func=AF.Square))
            C.op('tensor', lambda e, f=f: e.matmul(psR[:], lhsT=onescol_bf[:], rhs=sqb[:], start=(f == 0), stop=(f == 15)))
        C.op('scalar', lambda e: e.activation(out=rstd[:], in_=psR[:], func=AF.Ln, scale=1.0 / 2048.0, bias=l.cp[0:1, CP_EPS5:CP_EPS5 + 1]))
        C.op('scalar', lambda e: e.activation(out=rstd[:], in_=rstd[:], func=AF.Exp, scale=-0.5))
        C.op('tensor', lambda e: e.matmul(ps2[:], lhsT=l.ones[:, 0:128], rhs=rstd[:], start=True, stop=True))
        C.op('vector', lambda e: e.tensor_copy(rb_bf[:], ps2[:]))
        for f in range(16):
            C.op('vector', lambda e, f=f, n0=n0: e.tensor_tensor(out=gn[f][:, n0:n0 + 512], in0=gn[f][:, n0:n0 + 512], in1=rb_bf[:], op=OP.mult))
            C.op('vector', lambda e, f=f, n0=n0: e.tensor_scalar(out=gn[f][:, n0:n0 + 512], in0=gn[f][:, n0:n0 + 512], scalar1=rmw[:, f:f + 1], scalar2=None, op0=OP.mult))
    rstg = l.sb((128, 2048))
    sq2 = l.sb((128, 2048))
    rn = [l.sb((128, 1)) for _ in range(8)]
    for rt in range(8):
        C.dma(rstg[:], opnat[rt * 128:(rt + 1) * 128, :])
        C.op('scalar', lambda e: e.activation(out=sq2[:], in_=rstg[:], func=AF.Square))
        C.op('vector', lambda e, rt=rt: e.tensor_reduce(out=rn[rt][:], in_=sq2[:], axis=AX.X, op=OP.add))
        C.op('scalar', lambda e, rt=rt: e.activation(out=rn[rt][:], in_=rn[rt][:], func=AF.Sqrt))
        C.op('vector', lambda e, rt=rt: e.tensor_scalar(out=rn[rt][:], in0=rn[rt][:], scalar1=1e-6, scalar2=None, op0=OP.max))
        C.op('vector', lambda e, rt=rt: e.reciprocal(out=rn[rt][:], in_=rn[rt][:]))
    opb = [l.sb((128, D), BF16) for _ in range(16)]
    ostg = l.sb((128, D))
    for k in range(16):
        C.dma(ostg[:], opT[k * 128:(k + 1) * 128, :])
        C.op('vector', lambda e, k=k: e.tensor_copy(opb[k][:], ostg[:]))
    ps1 = l.ps((128, 512))
    psT = l.ps((128, 128))
    stage = l.sb((128, 512))
    tst = l.sb((128, 128))
    for m in range(8):
        for nch in range(4):
            n0 = nch * 512
            for k in range(16):
                C.op('tensor', lambda e, m=m, n0=n0, k=k: e.matmul(
                    ps1[:], lhsT=opb[k][:, m * 128:(m + 1) * 128], rhs=gn[k][:, n0:n0 + 512],
                    start=(k == 0), stop=(k == 15)))
            C.op('vector', lambda e, m=m: e.tensor_scalar(out=stage[:], in0=ps1[:], scalar1=rn[m][:], scalar2=None, op0=OP.mult))
            for i in range(4):
                tt = nch * 4 + i
                C.op('tensor', lambda e, i=i: e.transpose(psT[:], stage[:, i * 128:(i + 1) * 128], l.ident[:]))
                C.op('vector', lambda e: e.tensor_copy(tst[:], psT[:]))
                C.dma(s3o[tt * 128:(tt + 1) * 128, m * 128:(m + 1) * 128], tst[:])
    return l


FP4_MID = [0.25, 0.75, 1.25, 1.75, 2.5, 3.5, 5.0]
FP4_D = [0.5, 0.5, 0.5, 0.5, 1.0, 1.0, 2.0]


def build_I2a():
    l = L("I2a")
    C = l.C
    xn3 = l.din("xn3", (TPC, D))
    fg_nat = l.din("fg_nat", (2816, D))
    fgT = l.din("fgT", (D, 2816))
    fu_nat = l.din("fu_nat", (2816, D))
    fuT = l.din("fuT", (D, 2816))
    blkm_d = l.din("blkm", (8, 128))
    hqo = l.dout("hqo", (2816, TPC), BF16)
    l.common_consts()

    blkm = l.sb((8, 128))
    C.dma(blkm[:], blkm_d[:])
    ps1 = l.ps((128, 512))
    ps2 = l.ps((128, 512))
    psT = l.ps((128, 128))
    psTB = l.ps((128, 128), BF16)

    wq8 = [l.sb((128, 2816), BF16) for _ in range(8)]
    hq = [l.sb((128, TPC), BF16) for _ in range(22)]
    xqT = [l.sb((128, TPC), BF16) for _ in range(8)]
    xT32s = l.sb((128, 512))
    rsThA_full = l.sb((128, TPC))
    wstage = l.sb((128, 1024))
    rs_small = l.sb((128, 176))
    rsTw = l.sb((64, 2816))
    rsx4 = l.sb((64, 512))
    rsThA = rsThA_full
    rsThB = rsTw[0:64, 0:TPC]
    blk8 = l.sb((8, 512))
    xnb2 = l.sb((128, 512))
    axb = l.sb((128, 512))
    qm = l.sb((128, 512))
    tq = l.sb((128, 512))
    scl = tq
    rb_bf = l.sb((128, 512), BF16)

    def quant_cols(dst_ap, src_ap, cw):
        C.op('vector', lambda e: e.tensor_tensor(out=xnb2[:, 0:cw], in0=src_ap, in1=ps2[:, 0:cw], op=OP.mult))
        C.op('scalar', lambda e: e.activation(out=axb[:, 0:cw], in_=xnb2[:, 0:cw], func=AF.Abs))
        C.op('vector', lambda e: e.tensor_scalar(out=qm[:, 0:cw], in0=axb[:, 0:cw], scalar1=FP4_MID[0], scalar2=FP4_D[0], op0=OP.is_gt, op1=OP.mult))
        for mi in range(1, 7):
            C.op('vector', lambda e, mi=mi: e.tensor_scalar(out=tq[:, 0:cw], in0=axb[:, 0:cw], scalar1=FP4_MID[mi], scalar2=FP4_D[mi], op0=OP.is_gt, op1=OP.mult))
            C.op('vector', lambda e: e.tensor_tensor(out=qm[:, 0:cw], in0=qm[:, 0:cw], in1=tq[:, 0:cw], op=OP.add))
        C.op('scalar', lambda e: e.activation(out=axb[:, 0:cw], in_=xnb2[:, 0:cw], func=AF.Sign))
        C.op('vector', lambda e: e.tensor_tensor(out=qm[:, 0:cw], in0=qm[:, 0:cw], in1=axb[:, 0:cw], op=OP.mult))
        C.op('vector', lambda e: e.reciprocal(out=scl[:, 0:cw], in_=ps2[:, 0:cw]))
        C.op('vector', lambda e: e.tensor_tensor(out=dst_ap, in0=qm[:, 0:cw], in1=scl[:, 0:cw], op=OP.mult))
    l.quant_cols = quant_cols

    def wquant_gateup(nat_d, T_d):
        for rt in range(22):
            C.dma(wstage[:, 0:1024], nat_d[rt * 128:(rt + 1) * 128, :])
            C.op('vector', lambda e: e.tensor_reduce(out=rs_small[:, 0:64], in_=wstage[:, 0:1024].rearrange("p (b s) -> p b s", s=16), axis=AX.X, op=OP.max, apply_absolute_value=True))
            C.op('vector', lambda e: e.tensor_scalar(out=rs_small[:, 0:64], in0=rs_small[:, 0:64], scalar1=1.0 / 6.0, scalar2=1e-12, op0=OP.mult, op1=OP.max))
            C.op('vector', lambda e: e.reciprocal(out=rs_small[:, 0:64], in_=rs_small[:, 0:64]))
            C.op('tensor', lambda e: e.transpose(psT[0:64, 0:128], rs_small[:, 0:64], l.ident[:]))
            C.op('vector', lambda e, rt=rt: e.tensor_copy(rsTw[:, rt * 128:(rt + 1) * 128], psT[0:64, 0:128]))
        for f in range(8):
            for c6 in range(6):
                c0 = c6 * 512
                cw = 512 if c6 < 5 else 256
                C.dma(blk8[:, 0:cw], rsTw[f * 8:f * 8 + 8, c0:c0 + cw])
                C.op('tensor', lambda e, cw=cw: e.matmul(ps2[:, 0:cw], lhsT=blkm[:], rhs=blk8[:, 0:cw], start=True, stop=True))
                C.dma(wstage[:, 0:cw], T_d[f * 128:(f + 1) * 128, c0:c0 + cw])
                quant_cols(wq8[f][:, c0:c0 + cw], wstage[:, 0:cw], cw)

    for f in range(8):
        for nch in range(4):
            n0 = nch * 512
            for i in range(4):
                tt = nch * 4 + i
                C.dma(wstage[:, 0:1024], xn3[tt * 128:(tt + 1) * 128, :])
                C.op('vector', lambda e: e.tensor_reduce(out=rs_small[:, 0:64], in_=wstage[:, 0:1024].rearrange("p (b s) -> p b s", s=16), axis=AX.X, op=OP.max, apply_absolute_value=True))
                C.op('vector', lambda e: e.tensor_scalar(out=rs_small[:, 0:64], in0=rs_small[:, 0:64], scalar1=1.0 / 6.0, scalar2=1e-12, op0=OP.mult, op1=OP.max))
                C.op('vector', lambda e: e.reciprocal(out=rs_small[:, 0:64], in_=rs_small[:, 0:64]))
                C.op('tensor', lambda e: e.transpose(psT[0:64, 0:128], rs_small[:, 0:64], l.ident[:]))
                C.op('vector', lambda e, i=i: e.tensor_copy(rsx4[:, i * 128:(i + 1) * 128], psT[0:64, 0:128]))
                C.op('tensor', lambda e, f=f: e.transpose(psT[:], wstage[:, f * 128:(f + 1) * 128], l.ident[:]))
                C.op('vector', lambda e, i=i: e.tensor_copy(xT32s[:, i * 128:(i + 1) * 128], psT[:]))
            C.dma(blk8[:, 0:512], rsx4[f * 8:f * 8 + 8, :])
            C.op('tensor', lambda e: e.matmul(ps2[:, 0:512], lhsT=blkm[:], rhs=blk8[:, 0:512], start=True, stop=True))
            quant_cols(xqT[f][:, n0:n0 + 512], xT32s[:, 0:512], 512)
    wquant_gateup(fg_nat, fgT)
    for m in range(22):
        for nch in range(4):
            n0 = nch * 512
            for k in range(8):
                C.op('tensor', lambda e, m=m, n0=n0, k=k: e.matmul(
                    ps1[:], lhsT=wq8[k][:, m * 128:(m + 1) * 128], rhs=xqT[k][:, n0:n0 + 512],
                    start=(k == 0), stop=(k == 7)))
            C.op('scalar', lambda e, m=m, n0=n0: e.activation(out=hq[m][:, n0:n0 + 512], in_=ps1[:], func=AF.Silu))
    wquant_gateup(fu_nat, fuT)
    for m in range(22):
        for nch in range(4):
            n0 = nch * 512
            for k in range(8):
                C.op('tensor', lambda e, m=m, n0=n0, k=k: e.matmul(
                    ps1[:], lhsT=wq8[k][:, m * 128:(m + 1) * 128], rhs=xqT[k][:, n0:n0 + 512],
                    start=(k == 0), stop=(k == 7)))
            C.op('vector', lambda e: e.tensor_copy(rb_bf[:], ps1[:]))
            C.op('vector', lambda e, m=m, n0=n0: e.tensor_tensor(out=hq[m][:, n0:n0 + 512], in0=hq[m][:, n0:n0 + 512], in1=rb_bf[:], op=OP.mult))
    for i in range(NTT):
        for m in range(22):
            C.op('tensor', lambda e, m=m, i=i: e.transpose(psTB[:], hq[m][:, i * 128:(i + 1) * 128], l.identb[:]))
            C.op('vector', lambda e, m=m: e.tensor_reduce(out=rs_small[:, m * 8:(m + 1) * 8], in_=psTB[:].rearrange("p (b s) -> p b s", s=16), axis=AX.X, op=OP.max, apply_absolute_value=True))
        C.op('vector', lambda e: e.tensor_scalar(out=rs_small[:, 0:176], in0=rs_small[:, 0:176], scalar1=1.0 / 6.0, scalar2=1e-12, op0=OP.mult, op1=OP.max))
        C.op('vector', lambda e: e.reciprocal(out=rs_small[:, 0:176], in_=rs_small[:, 0:176]))
        C.op('tensor', lambda e: e.transpose(psT[:], rs_small[:, 0:128], l.ident[:]))
        C.op('vector', lambda e, i=i: e.tensor_copy(rsThA[:, i * 128:(i + 1) * 128], psT[:]))
        C.op('tensor', lambda e: e.transpose(psT[:], rs_small[:, 48:176], l.ident[:]))
        C.op('vector', lambda e, i=i: e.tensor_copy(rsThB[:, i * 128:(i + 1) * 128], psT[64:128, 0:128]))
    for f in range(22):
        r0 = f * 8
        for nch in range(4):
            n0 = nch * 512
            if r0 + 8 <= 128:
                C.dma(blk8[:, 0:512], rsThA[r0:r0 + 8, n0:n0 + 512])
            else:
                C.dma(blk8[:, 0:512], rsThB[r0 - 112:r0 - 112 + 8, n0:n0 + 512])
            C.op('tensor', lambda e: e.matmul(ps2[:, 0:512], lhsT=blkm[:], rhs=blk8[:, 0:512], start=True, stop=True))
            C.op('vector', lambda e: e.tensor_copy(rb_bf[:], ps2[:, 0:512]))
            C.op('vector', lambda e, f=f, n0=n0: e.tensor_tensor(out=xnb2[:, 0:512], in0=hq[f][:, n0:n0 + 512], in1=rb_bf[:], op=OP.mult))
            C.op('scalar', lambda e: e.activation(out=axb[:, 0:512], in_=xnb2[:, 0:512], func=AF.Abs))
            C.op('vector', lambda e: e.tensor_scalar(out=qm[:, 0:512], in0=axb[:, 0:512], scalar1=FP4_MID[0], scalar2=FP4_D[0], op0=OP.is_gt, op1=OP.mult))
            for mi in range(1, 7):
                C.op('vector', lambda e, mi=mi: e.tensor_scalar(out=tq[:, 0:512], in0=axb[:, 0:512], scalar1=FP4_MID[mi], scalar2=FP4_D[mi], op0=OP.is_gt, op1=OP.mult))
                C.op('vector', lambda e: e.tensor_tensor(out=qm[:, 0:512], in0=qm[:, 0:512], in1=tq[:, 0:512], op=OP.add))
            C.op('scalar', lambda e: e.activation(out=axb[:, 0:512], in_=xnb2[:, 0:512], func=AF.Sign))
            C.op('vector', lambda e: e.tensor_tensor(out=qm[:, 0:512], in0=qm[:, 0:512], in1=axb[:, 0:512], op=OP.mult))
            C.op('vector', lambda e: e.reciprocal(out=scl[:, 0:512], in_=ps2[:, 0:512]))
            C.op('vector', lambda e, f=f, n0=n0: e.tensor_tensor(out=hq[f][:, n0:n0 + 512], in0=qm[:, 0:512], in1=scl[:, 0:512], op=OP.mult))
        C.dma(hqo[f * 128:(f + 1) * 128, :], hq[f][:])
    return l


def build_I2b():
    l = L("I2b")
    C = l.C
    hq_d = l.din("hq", (2816, TPC), BF16)
    x3 = l.din("x3", (TPC, D))
    s3 = l.din("s3", (TPC, D))
    fd_nat = l.din("fd_nat", (D, 2816))
    fdT = l.din("fdT", (2816, D))
    blkm_d = l.din("blkm", (8, 128))
    x4o = l.dout("x4o", (TPC, D))
    l.common_consts()

    blkm = l.sb((8, 128))
    C.dma(blkm[:], blkm_d[:])
    ps1 = l.ps((128, 512))
    ps2 = l.ps((128, 512))
    psT = l.ps((128, 128))
    wdq = [l.sb((128, D), BF16) for _ in range(22)]
    hqt = [l.sb((128, 128), BF16) for _ in range(22)]
    wstage = l.sb((128, 2816))
    rs_small = l.sb((128, 176))
    rsTdA = l.sb((128, D))
    rsTdB = l.sb((64, D))
    blk8 = l.sb((8, D))
    xnb2 = l.sb((128, 512))
    axb = l.sb((128, 512))
    qm = l.sb((128, 512))
    tq = l.sb((128, 512))
    scl = l.sb((128, 512))
    g3s = l.sb((128, 512))
    xstg = l.sb((128, D))
    sstg = l.sb((128, D))

    for rt in range(8):
        C.dma(wstage[:, 0:2816], fd_nat[rt * 128:(rt + 1) * 128, :])
        C.op('vector', lambda e: e.tensor_reduce(out=rs_small[:, 0:176], in_=wstage[:, 0:2816].rearrange("p (b s) -> p b s", s=16), axis=AX.X, op=OP.max, apply_absolute_value=True))
        C.op('vector', lambda e: e.tensor_scalar(out=rs_small[:, 0:176], in0=rs_small[:, 0:176], scalar1=1.0 / 6.0, scalar2=1e-12, op0=OP.mult, op1=OP.max))
        C.op('vector', lambda e: e.reciprocal(out=rs_small[:, 0:176], in_=rs_small[:, 0:176]))
        C.op('tensor', lambda e: e.transpose(psT[:], rs_small[:, 0:128], l.ident[:]))
        C.op('vector', lambda e, rt=rt: e.tensor_copy(rsTdA[:, rt * 128:(rt + 1) * 128], psT[:]))
        C.op('tensor', lambda e: e.transpose(psT[:], rs_small[:, 48:176], l.ident[:]))
        C.op('vector', lambda e, rt=rt: e.tensor_copy(rsTdB[:, rt * 128:(rt + 1) * 128], psT[64:128, 0:128]))
    for f in range(22):
        r0 = f * 8
        if r0 + 8 <= 128:
            C.dma(blk8[:, 0:D], rsTdA[r0:r0 + 8, :])
        else:
            C.dma(blk8[:, 0:D], rsTdB[r0 - 112:r0 - 112 + 8, :])
        for nch in range(2):
            n0 = nch * 512
            C.op('tensor', lambda e, n0=n0: e.matmul(ps2[:, 0:512], lhsT=blkm[:], rhs=blk8[:, n0:n0 + 512], start=True, stop=True))
            C.dma(wstage[:, 0:512], fdT[f * 128:(f + 1) * 128, n0:n0 + 512])
            C.op('vector', lambda e: e.tensor_tensor(out=xnb2[:, 0:512], in0=wstage[:, 0:512], in1=ps2[:, 0:512], op=OP.mult))
            C.op('scalar', lambda e: e.activation(out=axb[:, 0:512], in_=xnb2[:, 0:512], func=AF.Abs))
            C.op('vector', lambda e: e.tensor_scalar(out=qm[:, 0:512], in0=axb[:, 0:512], scalar1=FP4_MID[0], scalar2=FP4_D[0], op0=OP.is_gt, op1=OP.mult))
            for mi in range(1, 7):
                C.op('vector', lambda e, mi=mi: e.tensor_scalar(out=tq[:, 0:512], in0=axb[:, 0:512], scalar1=FP4_MID[mi], scalar2=FP4_D[mi], op0=OP.is_gt, op1=OP.mult))
                C.op('vector', lambda e: e.tensor_tensor(out=qm[:, 0:512], in0=qm[:, 0:512], in1=tq[:, 0:512], op=OP.add))
            C.op('scalar', lambda e: e.activation(out=axb[:, 0:512], in_=xnb2[:, 0:512], func=AF.Sign))
            C.op('vector', lambda e: e.tensor_tensor(out=qm[:, 0:512], in0=qm[:, 0:512], in1=axb[:, 0:512], op=OP.mult))
            C.op('vector', lambda e: e.reciprocal(out=scl[:, 0:512], in_=ps2[:, 0:512]))
            C.op('vector', lambda e, f=f, n0=n0: e.tensor_tensor(out=wdq[f][:, n0:n0 + 512], in0=qm[:, 0:512], in1=scl[:, 0:512], op=OP.mult))
    for tt in range(NTT):
        for k in range(22):
            C.dma(hqt[k][:], hq_d[k * 128:(k + 1) * 128, tt * 128:(tt + 1) * 128])
        C.dma(xstg[:], x3[tt * 128:(tt + 1) * 128, :])
        C.dma(sstg[:], s3[tt * 128:(tt + 1) * 128, :])
        for nch in range(2):
            n0 = nch * 512
            for k in range(22):
                C.op('tensor', lambda e, k=k, n0=n0: e.matmul(
                    ps1[:], lhsT=hqt[k][:], rhs=wdq[k][:, n0:n0 + 512], start=(k == 0), stop=(k == 21)))
            C.op('scalar', lambda e: e.activation(out=g3s[:], in_=ps1[:], func=AF.Silu))
            C.op('vector', lambda e, n0=n0: e.tensor_tensor(out=g3s[:], in0=g3s[:], in1=sstg[:, n0:n0 + 512], op=OP.mult))
            C.op('vector', lambda e, n0=n0: e.tensor_tensor(out=xstg[:, n0:n0 + 512], in0=xstg[:, n0:n0 + 512], in1=g3s[:], op=OP.add))
        C.dma(x4o[tt * 128:(tt + 1) * 128, :], xstg[:])
    return l


def _consts():
    ident = np.eye(128, dtype=np.float32)
    import ml_dtypes
    identb = np.eye(128, dtype=ml_dtypes.bfloat16)
    ones = np.ones((1, 512), np.float32)
    onescol = np.ones((128, 2), np.float32)
    cpool = np.zeros((128, 8), np.float32)
    cpool[:, 0] = -30.0
    cpool[:, 1] = 1e-5
    cpool[:, 2] = 1.0
    cpool[:, 3] = 1e-12
    return dict(c_ident=ident, c_identb=identb, c_ones=ones, c_onescol=onescol, c_cpool=cpool)


def _pm_major(v):
    n = v.shape[0] // 128
    return np.ascontiguousarray(v.reshape(n, 128).T)


_BUILT = {}
DBG = {}


def _run(name, build_fn, in_maps):
    if name not in _BUILT:
        _BUILT[name] = build_fn().finish()
    nc = _BUILT[name]
    res = run_bass_kernel_spmd(nc, in_maps, core_ids=list(range(NC)))
    return res.results


def kernel(**inp):
    cs = _consts()
    f32 = np.float32
    x = np.ascontiguousarray(inp["L0_attn_x"], dtype=f32)
    xf = x.reshape(B * SEQ, D)

    def tok_slice(arr2d, c):
        return np.ascontiguousarray(arr2d[c * TPC:(c + 1) * TPC])

    T = lambda a: np.ascontiguousarray(np.asarray(a, dtype=f32).T)
    A = lambda a: np.ascontiguousarray(np.asarray(a, dtype=f32))

    l0m = dict(
        wqT=T(inp["L0_attn_Wq"]), bq=A(inp["L0_attn_bq"]).reshape(1, D),
        wkT=T(inp["L0_attn_Wk"]), bk=A(inp["L0_attn_bk"]).reshape(1, D),
        wvT=T(inp["L0_attn_Wv"]), bv=A(inp["L0_attn_bv"]).reshape(1, D),
        w1=A(inp["L0_ffn_W1"]), b1=A(inp["L0_ffn_b1"]).reshape(1, 512),
        w2=A(inp["L0_ffn_W2"]), b2=A(inp["L0_ffn_b2"]).reshape(1, 8),
        expW=A(inp["L0_ffn_expert_W"]), expb=A(inp["L0_ffn_expert_b"]))
    maps = [dict(xin=tok_slice(xf, c), **l0m, **cs) for c in range(NC)]
    rA = _run("A", build_A, maps)
    qT = np.concatenate([np.stack([r["qTo"] for r in rA[4 * b:4 * b + 4]], 0) for b in range(2)], 0)
    qTf = np.zeros((2, D, SEQ), f32); kTf = np.zeros((2, D, SEQ), f32)
    vf = np.zeros((2, SEQ, D), f32); g0f = np.zeros((B * SEQ, D), f32)
    for c in range(NC):
        b, o = c // 4, (c % 4) * TPC
        qTf[b, :, o:o + TPC] = rA[c]["qTo"]
        kTf[b, :, o:o + TPC] = rA[c]["kTo"]
        vf[b, o:o + TPC, :] = rA[c]["vo"]
        g0f[c * TPC:(c + 1) * TPC] = rA[c]["g0o"]

    DBG.update(g0f=g0f, qTf=qTf, kTf=kTf, vf=vf)
    l0masks = np.zeros((4, 128, 512), f32)
    for r in range(4):
        w_ = np.arange(128)[:, None]
        u_ = np.arange(512)[None, :]
        l0masks[r] = np.where(w_ + r * 128 <= u_, 0.0, -1e30)
    maps = []
    for c in range(NC):
        b, hb = c // 4, (c % 4) * 256
        maps.append(dict(qT4=np.ascontiguousarray(qTf[b, hb:hb + 256]),
                         kT4=np.ascontiguousarray(kTf[b, hb:hb + 256]),
                         v4=np.ascontiguousarray(vf[b, :, hb:hb + 256]),
                         l0masks=l0masks, **cs))
    rB = _run("B", build_B, maps)
    s0f = np.zeros((B * SEQ, D), f32)
    for c in range(NC):
        b, hb = c // 4, (c % 4) * 256
        s0f[b * SEQ:(b + 1) * SEQ, hb:hb + 256] = rB[c]["s0o"]

    DBG.update(s0f=s0f)
    invf = 1.0 / 10000.0 ** (np.arange(0, 64, 2) / 64.0)
    tpos = np.arange(SEQ)
    ang = tpos[:, None] * invf[None, :]
    angfull = np.concatenate([ang, ang], 1)
    cosT = np.cos(angfull).T
    sinT = np.sin(angfull).T
    cos128 = np.concatenate([cosT, cosT], 0).astype(f32)
    sin128 = np.concatenate([sinT, sinT], 0).astype(f32)
    pm = np.zeros((128, 128), f32)
    for dp in range(128):
        if dp % 2 == 0:
            pm[dp + 1, dp] = -1.0
        else:
            pm[dp - 1, dp] = 1.0
    l1m = dict(
        woT=T(inp["L0_attn_Wo"]), bo=A(inp["L0_attn_bo"]).reshape(1, D),
        wqT=T(inp["L1_attn_Wq"]), bq=A(inp["L1_attn_bq"]).reshape(1, D),
        wkT=T(inp["L1_attn_Wk"]), bk=A(inp["L1_attn_bk"]).reshape(1, D),
        wvT=T(inp["L1_attn_Wv"]), bv=A(inp["L1_attn_bv"]).reshape(1, D),
        wgT=T(inp["L1_attn_Wg"]), bg=A(inp["L1_attn_bg"]).reshape(1, D),
        pmrot=pm,
        fwg=np.ascontiguousarray(inp["L1_ffn_w_gate"].T).astype(np.int32),
        fwu=np.ascontiguousarray(inp["L1_ffn_w_up"].T).astype(np.int32),
        fwd=np.ascontiguousarray(inp["L1_ffn_w_down"].T).astype(np.int32))
    maps = []
    for c in range(NC):
        o = (c % 4) * TPC
        maps.append(dict(s0=tok_slice(s0f, c), xin=tok_slice(xf, c), g0=tok_slice(g0f, c),
                         costab=np.ascontiguousarray(cos128[:, o:o + TPC]),
                         sintab=np.ascontiguousarray(sin128[:, o:o + TPC]), **l1m, **cs))
    rC = _run("C", build_C, maps)
    x1f = np.zeros((B * SEQ, D), f32)
    q1Tf = np.zeros((2, D, SEQ), f32); k1Tf = np.zeros((2, D, SEQ), f32)
    v1f = np.zeros((2, SEQ, D), f32)
    gate1f = np.zeros((B * SEQ, D), f32); g1f = np.zeros((B * SEQ, D), f32)
    for c in range(NC):
        b, o = c // 4, (c % 4) * TPC
        x1f[c * TPC:(c + 1) * TPC] = rC[c]["x1o"]
        q1Tf[b, :, o:o + TPC] = rC[c]["q1o"]
        k1Tf[b, :, o:o + TPC] = rC[c]["k1o"]
        v1f[b, o:o + TPC, :] = rC[c]["v1o"]
        gate1f[c * TPC:(c + 1) * TPC] = rC[c]["gate1o"]
        g1f[c * TPC:(c + 1) * TPC] = rC[c]["g1o"]

    DBG.update(x1f=x1f, q1Tf=q1Tf, k1Tf=k1Tf, gate1f=gate1f, g1f=g1f)
    l1masks = np.zeros((4, 128, 512), f32)
    for r in range(4):
        w_ = np.arange(128)[:, None]
        u_ = np.arange(512)[None, :]
        l1masks[r] = np.maximum(u_ - w_ - r * 128, 0.0)
    iota2d = (np.arange(128)[:, None] + 128.0 * np.arange(64)[None, :]).astype(f32)
    gamma = A(inp["L1_attn_gamma"]).reshape(1, 16)
    maps = []
    for c in range(NC):
        b, hb = c // 4, (c % 4) * 256
        maps.append(dict(qT4=np.ascontiguousarray(q1Tf[b, hb:hb + 256]),
                         kT4=np.ascontiguousarray(k1Tf[b, hb:hb + 256]),
                         v4=np.ascontiguousarray(v1f[b, :, hb:hb + 256]),
                         gam=gamma, gam4=np.ascontiguousarray(gamma[:, (c % 4) * 4:(c % 4) * 4 + 4]),
                         l1masks=l1masks, iota2d=iota2d, **cs))
    rD = _run("D", build_D, maps)
    s1f = np.zeros((B * SEQ, D), f32)
    for c in range(NC):
        b, hb = c // 4, (c % 4) * 256
        s1f[b * SEQ:(b + 1) * SEQ, hb:hb + 256] = rD[c]["s1o"]

    DBG.update(s1f=s1f)
    l2m = dict(
        wq=A(inp["L2_attn_Wq"]), wk=A(inp["L2_attn_Wk"]), wv=A(inp["L2_attn_Wv"]),
        wgm=A(inp["L2_attn_Wgamma"]), wf1=A(inp["L2_attn_Wf1"]), wf2=A(inp["L2_attn_Wf2"]),
        wog1=A(inp["L2_attn_Wog1"]), wog2=A(inp["L2_attn_Wog2"]),
        fwg=np.ascontiguousarray(inp["L2_ffn_w_gate"].T).astype(np.int32),
        fwu=np.ascontiguousarray(inp["L2_ffn_w_up"].T).astype(np.int32),
        fwd=np.ascontiguousarray(inp["L2_ffn_w_down"].T).astype(np.int32))
    maps = [dict(x1=tok_slice(x1f, c), s1h=tok_slice(s1f, c), gate1=tok_slice(gate1f, c),
                 g1=tok_slice(g1f, c), **l2m, **cs) for c in range(NC)]
    rE = _run("E", build_E, maps)
    x2f = np.zeros((B * SEQ, D), f32)
    q2Tf = np.zeros((2, D, SEQ), f32); k2Tf = np.zeros((2, D, SEQ), f32); v2Tf = np.zeros((2, D, SEQ), f32)
    gk2Tf = np.zeros((2, D, SEQ), f32); gam2f = np.zeros((2, 16, SEQ), f32)
    og2f = np.zeros((B * SEQ, D), f32); g2f = np.zeros((B * SEQ, D), f32)
    for c in range(NC):
        b, o = c // 4, (c % 4) * TPC
        x2f[c * TPC:(c + 1) * TPC] = rE[c]["x2o"]
        q2Tf[b, :, o:o + TPC] = rE[c]["q2o"]
        k2Tf[b, :, o:o + TPC] = rE[c]["k2o"]
        v2Tf[b, :, o:o + TPC] = rE[c]["v2o"]
        gk2Tf[b, :, o:o + TPC] = rE[c]["gk2o"]
        gam2f[b, :, o:o + TPC] = rE[c]["gam2o"]
        og2f[c * TPC:(c + 1) * TPC] = rE[c]["og2o"]
        g2f[c * TPC:(c + 1) * TPC] = rE[c]["g2o"]

    DBG.update(x2f=x2f, q2Tf=q2Tf, k2Tf=k2Tf, v2Tf=v2Tf, gk2Tf=gk2Tf, gam2f=gam2f, og2f=og2f, g2f=g2f)
    cuml = (np.arange(128)[:, None] <= np.arange(128)[None, :]).astype(f32)
    maskls = (np.arange(128)[None, :] < np.arange(128)[:, None]).astype(f32)
    maps = []
    for c in range(NC):
        b, hb = c // 4, (c % 4) * 256
        hh = (c % 4) * 4
        maps.append(dict(qh4=np.ascontiguousarray(q2Tf[b, hb:hb + 256]),
                         kh4=np.ascontiguousarray(k2Tf[b, hb:hb + 256]),
                         vh4=np.ascontiguousarray(v2Tf[b, hb:hb + 256]),
                         gkh4=np.ascontiguousarray(gk2Tf[b, hb:hb + 256]),
                         gam4=np.ascontiguousarray(gam2f[b, hh:hh + 4]),
                         cuml=cuml, maskls=maskls, **cs))
    rF = _run("F", build_F, maps)
    o2f = np.zeros((B * SEQ, D), f32)
    for c in range(NC):
        b, hb = c // 4, (c % 4) * 256
        o2f[b * SEQ:(b + 1) * SEQ, hb:hb + 256] = rF[c]["o2o"]

    DBG.update(o2f=o2f)
    l3m = dict(
        wo=A(inp["L2_attn_Wo"]), normw=_pm_major(A(inp["L2_attn_norm_w"])),
        ipT=T(inp["L3_attn_in_proj_w"]), ipnat=A(inp["L3_attn_in_proj_w"]),
        xnw=_pm_major(A(inp["L3_attn_x_norm_w"])), xnb=_pm_major(A(inp["L3_attn_x_norm_b"])),
        dtb=A(inp["L3_attn_dt_bias"]).reshape(16, 1))
    maps = [dict(x2=tok_slice(x2f, c), o2=tok_slice(o2f, c), og2=tok_slice(og2f, c),
                 g2=tok_slice(g2f, c), **l3m, **cs) for c in range(NC)]
    rG = _run("G", build_G, maps)
    x3f = np.zeros((B * SEQ, D), f32); xn3f = np.zeros((B * SEQ, D), f32)
    zTf = np.zeros((2, 2048, SEQ), f32); xbcf = np.zeros((2, 2176, SEQ), f32)
    dtf = np.zeros((2, 16, SEQ), f32)
    for c in range(NC):
        b, o = c // 4, (c % 4) * TPC
        x3f[c * TPC:(c + 1) * TPC] = rG[c]["x3o"]
        xn3f[c * TPC:(c + 1) * TPC] = rG[c]["xn3o"]
        zTf[b, :, o:o + TPC] = rG[c]["zTo"]
        xbcf[b, :, o:o + TPC] = rG[c]["xbco"]
        dtf[b, :, o:o + TPC] = rG[c]["dtTo"]

    DBG.update(x3f=x3f, xn3f=xn3f, zTf=zTf, xbcf=xbcf, dtf=dtf)
    cw = A(inp["L3_attn_conv_w"])
    cb = A(inp["L3_attn_conv_b"]).reshape(2176, 1)
    alog = A(inp["L3_attn_A_log"]).reshape(1, 16)
    dd = A(inp["L3_attn_D"]).reshape(1, 16)
    maps = []
    for c in range(NC):
        b = c // 4
        hh = (c % 4) * 4
        rows = list(range(hh * 128, hh * 128 + 512))
        xb5 = np.concatenate([xbcf[b, hh * 128:hh * 128 + 512], xbcf[b, 2048:2176]], 0)
        cw5 = np.concatenate([cw[hh * 128:hh * 128 + 512], cw[2048:2176]], 0)
        cb5 = np.concatenate([cb[hh * 128:hh * 128 + 512], cb[2048:2176]], 0)
        maps.append(dict(xbc5=np.ascontiguousarray(xb5), cw5=np.ascontiguousarray(cw5),
                         cb5=np.ascontiguousarray(cb5),
                         dt4=np.ascontiguousarray(dtf[b, hh:hh + 4]),
                         arow4=np.ascontiguousarray(alog[:, hh:hh + 4]),
                         drow4=np.ascontiguousarray(dd[:, hh:hh + 4]),
                         cuml=cuml, **cs))
    rH = _run("H", build_H, maps)
    yTf = np.zeros((2, 2048, SEQ), f32)
    for c in range(NC):
        b = c // 4
        hh = (c % 4) * 4
        yTf[b, hh * 128:hh * 128 + 512] = rH[c]["yTo"]

    DBG.update(yTf=yTf)
    i1m = dict(rmsw=_pm_major(A(inp["L3_attn_rms_w"])),
               opT=T(inp["L3_attn_out_proj_w"]), opnat=A(inp["L3_attn_out_proj_w"]))
    maps = []
    for c in range(NC):
        b, o = c // 4, (c % 4) * TPC
        maps.append(dict(zT=np.ascontiguousarray(zTf[b, :, o:o + TPC]),
                         yT=np.ascontiguousarray(yTf[b, :, o:o + TPC]), **i1m, **cs))
    rI1 = _run("I1", build_I1, maps)
    s3f = np.zeros((B * SEQ, D), f32)
    for c in range(NC):
        s3f[c * TPC:(c + 1) * TPC] = rI1[c]["s3o"]

    DBG.update(s3f=s3f)
    blkm = np.zeros((8, 128), f32)
    for p in range(128):
        blkm[p // 16, p] = 1.0
    i2m = dict(fg_nat=A(inp["L3_ffn_w_gate"]), fgT=T(inp["L3_ffn_w_gate"]),
               fu_nat=A(inp["L3_ffn_w_up"]), fuT=T(inp["L3_ffn_w_up"]),
               fd_nat=A(inp["L3_ffn_w_down"]), fdT=T(inp["L3_ffn_w_down"]),
               blkm=blkm)
    maps = [dict(xn3=tok_slice(xn3f, c), fg_nat=i2m["fg_nat"], fgT=i2m["fgT"],
                 fu_nat=i2m["fu_nat"], fuT=i2m["fuT"], blkm=blkm, **cs) for c in range(NC)]
    rI2a = _run("I2a", build_I2a, maps)
    maps = [dict(hq=rI2a[c]["hqo"], x3=tok_slice(x3f, c), s3=tok_slice(s3f, c),
                 fd_nat=i2m["fd_nat"], fdT=i2m["fdT"], blkm=blkm, **cs) for c in range(NC)]
    rI2b = _run("I2b", build_I2b, maps)
    x4 = np.zeros((B * SEQ, D), f32)
    for c in range(NC):
        x4[c * TPC:(c + 1) * TPC] = rI2b[c]["x4o"]
    return x4.reshape(B, SEQ, D)
